# version 18
# baseline (speedup 1.0000x reference)
import math
from contextlib import ExitStack
import numpy as np
import ml_dtypes

import concourse.bass as bass
import concourse.mybir as mybir
from concourse.bass_utils import run_bass_kernel_spmd

F32 = mybir.dt.float32
BF16 = mybir.dt.bfloat16
AF = mybir.ActivationFunctionType
ALU = mybir.AluOpType
AX = mybir.AxisListType
bfdt = ml_dtypes.bfloat16

NCORES = 8
TOK = 1024
TC = TOK // 128
TH = 2
NT = TOK // TH
D = 2048
DC = D // 128
HEADS = 16
L = 64
HID = 8192
HC = HID // 128

RSQ_HD = 1.0 / math.sqrt(128.0)
RSQ_HID = 1.0 / math.sqrt(8192.0)
EPS = 1e-5
NEG = -1e30
QV_SERIAL = 4


class Ctr:
    def __init__(self, nc, name, step=1):
        self.h = nc.semaphore(name).__enter__()
        self.step = step
        self.n = 0

    def inc(self, instr):
        instr.then_inc(self.h, self.step)
        self.n += 1
        return self.n * self.step

    def bump(self):
        self.n += 1
        return self.n * self.step

    @property
    def val(self):
        return self.n * self.step


def butterfly_stages():
    out = []
    h = 1
    while h < HC:
        stage = []
        for g in range(0, HC, 2 * h):
            for j in range(h):
                stage.append((g + j, g + h + j))
        out.append(stage)
        h *= 2
    return out


def hadamard(n):
    h = np.array([[1.0]])
    while h.shape[0] < n:
        h = np.block([[h, h], [h, -h]])
    return h


def emit_ln_stats(ab, scr, pstatA, pstatB, sV, sS):
    A = ab[0:1, 0:TOK]
    B = ab[0:1, TOK:2 * TOK]
    M = scr[0:1, 0:TOK]
    E = scr[0:1, TOK:2 * TOK]

    def f1(v):
        v.tensor_scalar(M, pstatA[0:1, :], 1.0 / D, None, op0=ALU.mult)
        v.tensor_scalar(E, pstatB[0:1, :], 1.0 / D, None, op0=ALU.mult)
        v.tensor_tensor(B, M, M, op=ALU.mult)
        v.tensor_tensor(E, E, B, op=ALU.subtract)
        ins = v.tensor_scalar(E, E, EPS, None, op0=ALU.add)
        return sV.inc(ins)

    def f2(s):
        ins = s.activation(B, E, AF.Sqrt)
        return sS.inc(ins)

    def f3(v):
        ins = v.reciprocal(A, B)
        sV.inc(ins)
        v.wait_ge(sV.h, sV.val)
        v.tensor_tensor(B, A, A, op=ALU.mult)
        v.tensor_tensor(B, B, E, op=ALU.mult)
        v.tensor_scalar(B, B, -0.5, 1.5, op0=ALU.mult, op1=ALU.add)
        v.tensor_tensor(A, A, B, op=ALU.mult)
        v.tensor_tensor(B, M, A, op=ALU.mult)
        ins = v.tensor_scalar(B, B, -1.0, None, op0=ALU.mult)
        return sV.inc(ins)

    return f1, f2, f3


def build_nc(dbg=False, stage=None):
    nc = bass.Bass("TRN2", target_bir_lowering=False, debug=False,
                   detect_race_conditions=False)

    xT = nc.dram_tensor("xT", [D, TOK], F32, kind="ExternalInput").ap()
    wq = nc.dram_tensor("wq", [DC, 128, DC, 128], F32, kind="ExternalInput").ap()
    wv = nc.dram_tensor("wv", [DC, 128, DC, 128], F32, kind="ExternalInput").ap()
    wo = nc.dram_tensor("wo", [DC, 128, DC, 128], F32, kind="ExternalInput").ap()
    w1 = nc.dram_tensor("w1", [HC, 128, DC, 128], F32, kind="ExternalInput").ap()
    w3 = nc.dram_tensor("w3", [HC, 128, DC, 128], F32, kind="ExternalInput").ap()
    w2 = nc.dram_tensor("w2", [DC, 128, HC, 128], F32, kind="ExternalInput").ap()
    latT = nc.dram_tensor("latT", [128, DC * L], F32, kind="ExternalInput").ap()
    latN = nc.dram_tensor("latN", [L, D], F32, kind="ExternalInput").ap()
    bia = nc.dram_tensor("bia", [128, 48], F32, kind="ExternalInput").ap()
    h128h = nc.dram_tensor("h128c", [128, 128], BF16, kind="ExternalInput").ap()
    identh = nc.dram_tensor("identc", [128, 128], BF16, kind="ExternalInput").ap()
    onesbh = nc.dram_tensor("onesbc", [128, 1], BF16, kind="ExternalInput").ap()
    onesfh = nc.dram_tensor("onesfc", [1, 128], F32, kind="ExternalInput").ap()
    spk = "ExternalOutput" if dbg else "Internal"
    x1sp = nc.dram_tensor("x1sp", [D, TOK], F32, kind=spk).ap()
    r2sp = nc.dram_tensor("r2sp", [D, TOK], F32, kind=spk).ap()
    outT = nc.dram_tensor("outT", [D, TOK], F32, kind="ExternalOutput").ap()

    BST = butterfly_stages()

    with ExitStack() as _root:
        ec = _root.enter_context
        xb = ec(nc.sbuf_tensor("xb", [128, DC * TOK], BF16))
        onesb = ec(nc.sbuf_tensor("onesb", [128, 1], BF16))
        onesf = ec(nc.sbuf_tensor("onesf", [1, 128], F32))
        ab = ec(nc.sbuf_tensor("ab", [1, 2 * TOK], F32))
        w13w = ec(nc.sbuf_tensor("w13w", [128, 4 * D], BF16))
        w2w = ec(nc.sbuf_tensor("w2w", [128, 3 * 1024], BF16))
        h128s = ec(nc.sbuf_tensor("h128s", [128, 128], BF16))
        blk = ec(nc.Block())
        sT = Ctr(nc, "sT")
        sV = Ctr(nc, "sV")
        sS = Ctr(nc, "sS")
        sPRO = Ctr(nc, "sPRO", 16)
        sYP = Ctr(nc, "sYP", 16)
        qvS = [Ctr(nc, f"qv{i}", 16) for i in range(4)]
        w13S = [Ctr(nc, f"w13{i}", 16) for i in range(4)]
        w2S = [Ctr(nc, f"w2{i}", 16) for i in range(3)]
        sX1O = Ctr(nc, "sX1O", 16)
        sR2O = Ctr(nc, "sR2O", 16)
        sH128 = Ctr(nc, "sH128", 16)
        x1iS = [Ctr(nc, f"x1i{i}", 16) for i in range(4)]
        r2iS = [Ctr(nc, f"r2i{i}", 16) for i in range(2)]
        sOUT = Ctr(nc, "sOUT", 16)

        x1_cast = [None] * (TH * DC)
        norm1_done = [None] * (TH * DC)
        last_attn_T = [None]

        with ExitStack() as _attn:
            ea = _attn.enter_context
            wvT = ea(nc.sbuf_tensor("wvT", [128, DC * TOK], BF16))
            r1 = ea(nc.sbuf_tensor("r1", [128, DC * TOK], F32))
            qtr = ea(nc.sbuf_tensor("qtr", [128, 2 * TOK], BF16))
            lgs = ea(nc.sbuf_tensor("lgs", [128, TC * L], F32))
            msk = ea(nc.sbuf_tensor("msk", [128, TC * L], F32))
            m8s = ea(nc.sbuf_tensor("m8s", [128, TC * 8], F32))
            den = ea(nc.sbuf_tensor("den", [128, TC], F32))
            den2 = ea(nc.sbuf_tensor("den2", [128, TC], F32))
            wsm = ea(nc.sbuf_tensor("wsm", [128, TC * L], BF16))
            wTr = ea(nc.sbuf_tensor("wTr", [L, 2 * TOK], BF16))
            qvw = ea(nc.sbuf_tensor("qvw", [128, 3 * D], BF16))
            rbt = ea(nc.sbuf_tensor("rbt", [128, NT], BF16))
            sqt = ea(nc.sbuf_tensor("sqt", [128, NT], BF16))
            x1t = ea(nc.sbuf_tensor("x1t", [128, TOK], F32))
            lnscr = ea(nc.sbuf_tensor("lnscr", [1, 2 * TOK], F32))
            latTs = ea(nc.sbuf_tensor("latTs", [128, DC * L], BF16))
            latns = ea(nc.sbuf_tensor("latns", [L, D], BF16))
            idents = ea(nc.sbuf_tensor("idents", [128, 128], BF16))
            biass = ea(nc.sbuf_tensor("biass", [128, 48], F32))
            bqs = ea(nc.sbuf_tensor("bqs", [128, 16], F32))
            @blk.gpsimd
            def _(g):
                g.dma_start(latTs[:, :], latT[:, :]).then_inc(sPRO.h, 16); sPRO.bump()
                g.dma_start(latns[:, :], latN[:, :]).then_inc(sPRO.h, 16); sPRO.bump()
                for c in range(DC):
                    g.dma_start(xb[:, c * TOK:(c + 1) * TOK],
                                xT[c * 128:(c + 1) * 128, :]).then_inc(sPRO.h, 16)
                    sPRO.bump()
                g.dma_start(xb[:, 0:TOK],
                            xT[0:128, :]).then_inc(sPRO.h, 16)
                sPRO.bump()
            pro_all = sPRO.val

            @blk.sync
            def _(y):
                y.dma_start(idents[:, :], identh[:, :]).then_inc(sYP.h, 16); sYP.bump()
                y.dma_start(h128s[:, :], h128h[:, :]).then_inc(sH128.h, 16); sH128.bump()
                y.dma_start(biass[:, :], bia[:, :]).then_inc(sYP.h, 16); sYP.bump()
                y.dma_start(onesb[:, :], onesbh[:, :]).then_inc(sYP.h, 16); sYP.bump()
                y.dma_start(onesf[:, :], onesfh[:, :]).then_inc(sYP.h, 16); sYP.bump()
                for c in range(DC):
                    y.dma_start(r1[:, c * TOK:(c + 1) * TOK],
                                xT[c * 128:(c + 1) * 128, :]).then_inc(sYP.h, 16)
                    sYP.bump()
            yp_all = sYP.val

            @blk.vector
            def _(v):
                v.wait_ge(sYP.h, 16 * 2)
                ins = v.tensor_scalar(bqs[:, :], biass[:, 0:16], RSQ_HD, None, op0=ALU.mult)
                sV.inc(ins)
            bqs_ready = sV.val

            qv_src = []
            for h in range(HEADS):
                qv_src.append(wq[h])
                qv_src.append(wv[h])
            for oc in range(DC):
                qv_src.append(wo[oc])
            qv_last_use = [None] * len(qv_src)

            def qv_slot(j):
                return qvw[:, (j % 3) * D:((j % 3) + 1) * D]

            qt_ev = [None] * HEADS
            qt_ev_th = [[None] * TH for _ in range(HEADS)]
            q_done_th = [[None] * TH for _ in range(HEADS)]
            lg_done = [None] * HEADS
            lg_copy = [None] * HEADS
            top4_done = [None] * HEADS
            wt_ev = [None] * HEADS
            wv_ev = [None] * HEADS
            q_done = [None] * HEADS
            wtr_done = [None] * HEADS
            wgt_done = [None] * HEADS

            o_done = [[None] * TH for _ in range(DC)]
            r1_add = [[None] * TH for _ in range(DC)]
            r1_cast = [[None] * TH for _ in range(DC)]
            sq_done = [[None] * TH for _ in range(DC)]
            st_sum = [[None] * TH for _ in range(DC)]
            st_sq = [[None] * TH for _ in range(DC)]

            with nc.psum_tensor("pqo", [128, TOK], F32) as pqo:
                with ExitStack() as _hp:
                    pv = _hp.enter_context(nc.psum_tensor("pv", [128, 2 * TOK], F32))
                    plg = _hp.enter_context(nc.psum_tensor("plg", [128, 512], F32))
                    pwt = _hp.enter_context(nc.psum_tensor("pwt", [L, TOK], BF16))
                    def emit_q(t, h):
                        j = 2 * h
                        t.wait_ge(qvS[(j + 1) % 4].h, 16 * ((j + 1) // 4 + 1))
                        for th in range(TH):
                            if h >= 1:
                                t.wait_ge(sS.h, qt_ev_th[h - 1][th])
                            for kc in range(DC):
                                ins = t.matmul(
                                    pqo[:, th * NT:(th + 1) * NT],
                                    qv_slot(j)[:, kc * 128:(kc + 1) * 128],
                                    xb[:, kc * TOK + th * NT: kc * TOK + (th + 1) * NT],
                                    start=(kc == 0), stop=(kc == DC - 1))
                            sT.inc(ins)
                            q_done_th[h][th] = sT.val
                        q_done[h] = sT.val
                        qv_last_use[j] = sT.val

                    def emit_qt_evac(s, h):
                        if h == 0:
                            s.wait_ge(sV.h, bqs_ready)
                        if h >= 2:
                            s.wait_ge(sT.h, lg_done[h - 2])
                        qs = qtr[:, (h % 2) * TOK:((h % 2) + 1) * TOK]
                        for th in range(TH):
                            s.wait_ge(sT.h, q_done_th[h][th])
                            ins = s.activation(qs[:, th * NT:(th + 1) * NT],
                                               pqo[:, th * NT:(th + 1) * NT],
                                               AF.Identity, bias=bqs[:, h:h + 1],
                                               scale=RSQ_HD)
                            sS.inc(ins)
                            qt_ev_th[h][th] = sS.val
                        qt_ev[h] = sS.val

                    def emit_logits(t, h):
                        t.wait_ge(sS.h, qt_ev[h])
                        if h >= 1:
                            t.wait_ge(sV.h, lg_copy[h - 1])
                        qs = qtr[:, (h % 2) * TOK:((h % 2) + 1) * TOK]
                        for tc in range(TC):
                            ins = t.matmul(plg[:, tc * L:(tc + 1) * L],
                                           qs[:, tc * 128:(tc + 1) * 128],
                                           latTs[:, h * L:(h + 1) * L],
                                           start=True, stop=True)
                        sT.inc(ins)
                        lg_done[h] = sT.val

                    def emit_v(t, h):
                        j = 2 * h + 1
                        t.wait_ge(qvS[(j + 1) % 4].h, 16 * ((j + 1) // 4 + 1))
                        if h >= 2:
                            t.wait_ge(sS.h, wv_ev[h - 2])
                        pvb = (h % 2) * TOK
                        for th in range(TH):
                            for kc in range(DC):
                                ins = t.matmul(
                                    pv[:, pvb + th * NT: pvb + (th + 1) * NT],
                                    qv_slot(j)[:, kc * 128:(kc + 1) * 128],
                                    xb[:, kc * TOK + th * NT: kc * TOK + (th + 1) * NT],
                                    start=(kc == 0), stop=False)
                            sT.inc(ins)
                        qv_last_use[j] = sT.val

                    def emit_top4_pre(v, h):
                        v.wait_ge(sT.h, lg_done[h])
                        ins = v.tensor_copy(lgs[:, :], plg[:, :])
                        sV.inc(ins)
                        lg_copy[h] = sV.val
                        lg3 = lgs.rearrange("p (c l) -> p c l", l=L)
                        mk3 = msk.rearrange("p (c l) -> p c l", l=L)
                        for tc in range(TC):
                            v.max(m8s[:, tc * 8:(tc + 1) * 8], lg3[:, tc, :])
                        for tc in range(TC):
                            v.memset(m8s[:, tc * 8 + 4:tc * 8 + 8], NEG)
                        for tc in range(TC):
                            v.match_replace(mk3[:, tc, :], m8s[:, tc * 8:(tc + 1) * 8],
                                            lg3[:, tc, :], NEG)
                        m1 = m8s.rearrange("p (c k) -> p c k", k=8)[:, :, 0:1]
                        ins = v.tensor_tensor(lg3, lg3,
                                              m1.to_broadcast([128, TC, L]),
                                              op=ALU.subtract)
                        return sV.inc(ins)

                    def emit_exp(s, sub_val):
                        s.wait_ge(sV.h, sub_val)
                        ins = s.activation(lgs[:, :], lgs[:, :], AF.Exp)
                        sS.inc(ins)
                        return sS.val

                    def emit_top4_post(v, h, exp_val):
                        v.wait_ge(sS.h, exp_val)
                        if h >= 1:
                            v.wait_ge(sT.h, wtr_done[h - 1])
                        lg3 = lgs.rearrange("p (c l) -> p c l", l=L)
                        v.tensor_scalar(msk[:, :], msk[:, :], NEG, None, op0=ALU.is_equal)
                        ins = v.tensor_tensor(lgs[:, :], lgs[:, :], msk[:, :], op=ALU.mult)
                        sV.inc(ins)
                        v.wait_ge(sV.h, sV.val)
                        ins = v.tensor_reduce(den[:, :], lg3, axis=AX.X, op=ALU.add)
                        sV.inc(ins)
                        v.wait_ge(sV.h, sV.val)
                        ins = v.reciprocal(den2[:, :], den[:, :])
                        sV.inc(ins)
                        v.wait_ge(sV.h, sV.val)
                        d3 = den2.rearrange("p (c o) -> p c o", o=1)
                        ins = v.tensor_tensor(
                            wsm.rearrange("p (c l) -> p c l", l=L),
                            lg3, d3.to_broadcast([128, TC, L]), op=ALU.mult)
                        sV.inc(ins)
                        top4_done[h] = sV.val

                    def emit_wtrans(t, h):
                        t.wait_ge(sV.h, top4_done[h])
                        if h >= 1:
                            t.wait_ge(sS.h, wt_ev[h - 1])
                        for tc in range(TC):
                            ins = t.transpose(pwt[:, tc * 128:(tc + 1) * 128],
                                              wsm[:, tc * L:(tc + 1) * L],
                                              idents[:, :])
                        sT.inc(ins)
                        wtr_done[h] = sT.val

                    def emit_wt_evac(s, h):
                        s.wait_ge(sT.h, wtr_done[h])
                        if h >= 2:
                            s.wait_ge(sT.h, wgt_done[h - 2])
                        ws = wTr[:, (h % 2) * TOK:((h % 2) + 1) * TOK]
                        ins = s.activation(ws[:, :], pwt[:, :], AF.Identity)
                        sS.inc(ins)
                        wt_ev[h] = sS.val

                    def emit_weighted(t, h):
                        t.wait_ge(sS.h, wt_ev[h])
                        ws = wTr[:, (h % 2) * TOK:((h % 2) + 1) * TOK]
                        pvb = (h % 2) * TOK
                        for th in range(TH):
                            ins = t.matmul(pv[:, pvb + th * NT: pvb + (th + 1) * NT],
                                           latns[:, h * 128:(h + 1) * 128],
                                           ws[:, th * NT:(th + 1) * NT],
                                           start=False, stop=True)
                        sT.inc(ins)
                        wgt_done[h] = sT.val

                    def emit_wv_evac(s, h):
                        s.wait_ge(sT.h, wgt_done[h])
                        pvb = (h % 2) * TOK
                        for th in range(TH):
                            ins = s.activation(
                                wvT[:, h * TOK + th * NT: h * TOK + (th + 1) * NT],
                                pv[:, pvb + th * NT: pvb + (th + 1) * NT],
                                AF.Identity, bias=biass[:, 16 + h:17 + h])
                        sS.inc(ins)
                        wv_ev[h] = sS.val

                    @blk.tensor
                    def _(t):
                        t.wait_ge(sPRO.h, pro_all)
                        t.wait_ge(sYP.h, 16 * 1)
                        emit_q(t, 0)

                    @blk.scalar
                    def _(s):
                        emit_qt_evac(s, 0)

                    @blk.tensor
                    def _(t):
                        emit_logits(t, 0)

                    HR = 1 if stage == "h0" else HEADS
                    for h in range(HR):
                        sub_val = [None]

                        @blk.vector
                        def _(v, h=h):
                            sub_val[0] = emit_top4_pre(v, h)

                        @blk.tensor
                        def _(t, h=h):
                            emit_v(t, h)
                            if h + 1 < HR:
                                emit_q(t, h + 1)

                        exp_val = [None]

                        @blk.scalar
                        def _(s, h=h):
                            exp_val[0] = emit_exp(s, sub_val[0])
                            if h + 1 < HR:
                                emit_qt_evac(s, h + 1)
                            if h >= 1:
                                emit_wv_evac(s, h - 1)

                        @blk.vector
                        def _(v, h=h):
                            emit_top4_post(v, h, exp_val[0])

                        if h + 1 < HR:
                            @blk.tensor
                            def _(t, h=h):
                                emit_logits(t, h + 1)

                        @blk.tensor
                        def _(t, h=h):
                            emit_wtrans(t, h)

                        @blk.scalar
                        def _(s, h=h):
                            emit_wt_evac(s, h)

                        @blk.tensor
                        def _(t, h=h):
                            emit_weighted(t, h)

                    @blk.scalar
                    def _(s):
                        emit_wv_evac(s, HR - 1)

                if stage == "h0":
                    dq = nc.dram_tensor("dq", [128, TOK], BF16, kind="ExternalOutput").ap()
                    dlg = nc.dram_tensor("dlg", [128, TC * L], F32, kind="ExternalOutput").ap()
                    dws = nc.dram_tensor("dws", [128, TC * L], BF16, kind="ExternalOutput").ap()
                    dwt = nc.dram_tensor("dwt", [L, TOK], BF16, kind="ExternalOutput").ap()
                    dwv0 = nc.dram_tensor("dwv0", [128, TOK], BF16, kind="ExternalOutput").ap()
                    sDBG = Ctr(nc, "sDBG", 16)

                    @blk.sync
                    def _(y):
                        y.wait_ge(sS.h, wv_ev[0])
                        y.dma_start(dq[:, :], qtr[:, 0:TOK]).then_inc(sDBG.h, 16)
                        y.dma_start(dlg[:, :], lgs[:, :]).then_inc(sDBG.h, 16)
                        y.dma_start(dws[:, :], wsm[:, :]).then_inc(sDBG.h, 16)
                        y.dma_start(dwt[:, :], wTr[:, 0:TOK]).then_inc(sDBG.h, 16)
                        y.dma_start(dwv0[:, :], wvT[:, 0:TOK]).then_inc(sDBG.h, 16)
                        y.wait_ge(sDBG.h, 80)

                    @blk.gpsimd
                    def _(g):
                        for j, src_ in enumerate(qv_src[:2]):
                            g.dma_start(qv_slot(j), src_.rearrange("p c o -> p (c o)")
                                        ).then_inc(qvS[j % 4].h, 16)
                            qvS[j % 4].bump()
                    return nc

                with ExitStack() as _st:
                    pstatA = _st.enter_context(nc.psum_tensor("pstatA", [1, TOK], F32))
                    pstatB = _st.enter_context(nc.psum_tensor("pstatB", [1, TOK], F32))

                    for oc in range(DC):
                        @blk.tensor
                        def _(t, oc=oc):
                            j = 32 + oc
                            t.wait_ge(qvS[(j + 1) % 4].h, 16 * ((j + 1) // 4 + 1))
                            if oc == 0:
                                t.wait_ge(sS.h, wv_ev[HEADS - 1])
                            else:
                                t.wait_ge(sV.h, r1_add[oc - 1][TH - 1])
                            for th in range(TH):
                                for kc in range(DC):
                                    ins = t.matmul(
                                        pqo[:, th * NT:(th + 1) * NT],
                                        qv_slot(j)[:, kc * 128:(kc + 1) * 128],
                                        wvT[:, kc * TOK + th * NT: kc * TOK + (th + 1) * NT],
                                        start=(kc == 0), stop=(kc == DC - 1))
                                sT.inc(ins)
                                o_done[oc][th] = sT.val
                            qv_last_use[j] = sT.val

                        @blk.vector
                        def _(v, oc=oc):
                            if oc == 0:
                                v.wait_ge(sYP.h, yp_all)
                            for th in range(TH):
                                v.wait_ge(sT.h, o_done[oc][th])
                                sl = r1[:, oc * TOK + th * NT: oc * TOK + (th + 1) * NT]
                                ins = v.scalar_tensor_tensor(
                                    sl, pqo[:, th * NT:(th + 1) * NT],
                                    biass[:, 32 + oc:33 + oc], sl,
                                    op0=ALU.add, op1=ALU.add)
                                sV.inc(ins)
                                r1_add[oc][th] = sV.val

                        for th in range(TH):
                            @blk.scalar
                            def _(s, oc=oc, th=th):
                                k = oc * TH + th
                                s.wait_ge(sV.h, r1_add[oc][th])
                                if k >= 1:
                                    s.wait_ge(sT.h, st_sum[(k - 1) // TH][(k - 1) % TH])
                                    s.wait_ge(sV.h, sq_done[(k - 1) // TH][(k - 1) % TH])
                                ins = s.activation(
                                    rbt[:, :],
                                    r1[:, oc * TOK + th * NT: oc * TOK + (th + 1) * NT],
                                    AF.Identity)
                                sS.inc(ins)
                                r1_cast[oc][th] = sS.val

                            @blk.vector
                            def _(v, oc=oc, th=th):
                                k = oc * TH + th
                                v.wait_ge(sS.h, r1_cast[oc][th])
                                if k >= 1:
                                    v.wait_ge(sT.h, st_sq[(k - 1) // TH][(k - 1) % TH])
                                rb = rbt[:, :]
                                ins = v.tensor_tensor(sqt[:, :],
                                                      rb, rb, op=ALU.mult)
                                sV.inc(ins)
                                sq_done[oc][th] = sV.val

                            @blk.tensor
                            def _(t, oc=oc, th=th):
                                t.wait_ge(sS.h, r1_cast[oc][th])
                                ins = t.matmul(pstatA[0:1, th * NT:(th + 1) * NT],
                                               onesb[:, :],
                                               rbt[:, :],
                                               start=(oc == 0), stop=(oc == DC - 1))
                                sT.inc(ins)
                                st_sum[oc][th] = sT.val
                                t.wait_ge(sV.h, sq_done[oc][th])
                                ins = t.matmul(pstatB[0:1, th * NT:(th + 1) * NT],
                                               onesb[:, :],
                                               sqt[:, :],
                                               start=(oc == 0), stop=(oc == DC - 1))
                                sT.inc(ins)
                                st_sq[oc][th] = sT.val
                                last_attn_T[0] = sT.val

                    ln1 = emit_ln_stats(ab, lnscr, pstatA, pstatB, sV, sS)

                    @blk.vector
                    def _(v):
                        v.wait_ge(sT.h, st_sq[DC - 1][TH - 1])
                        ln1[0](v)

                    @blk.scalar
                    def _(s):
                        s.wait_ge(sV.h, sV.val)
                        ln1[1](s)

                    @blk.vector
                    def _(v):
                        v.wait_ge(sS.h, sS.val)
                        ln1[2](v)
                    ln1_ab = sV.val

                    with ExitStack() as _bc:
                        pbca = _bc.enter_context(nc.psum_tensor("pbca", [128, NT], F32))
                        pbcb = _bc.enter_context(nc.psum_tensor("pbcb", [128, NT], F32))
                        bc1 = [None, None]
                        for th in range(TH):
                            @blk.tensor
                            def _(t, th=th):
                                if th == 0:
                                    t.wait_ge(sV.h, ln1_ab)
                                else:
                                    t.wait_ge(sV.h, norm1_done[DC - 1])
                                t.matmul(pbca[:, :], onesf[0:1, :],
                                         ab[0:1, th * NT:(th + 1) * NT],
                                         start=True, stop=True)
                                ins = t.matmul(pbcb[:, :], onesf[0:1, :],
                                               ab[0:1, TOK + th * NT: TOK + (th + 1) * NT],
                                               start=True, stop=True)
                                sT.inc(ins)
                                bc1[th] = sT.val

                            for c in range(DC):
                                k = th * DC + c

                                @blk.vector
                                def _(v, th=th, c=c, k=k):
                                    if c == 0:
                                        v.wait_ge(sT.h, bc1[th])
                                    slot = k % 2
                                    if k >= 2:
                                        v.wait_ge(sX1O.h, 16 * (k - 1))
                                        v.wait_ge(sS.h, x1_cast[k - 2])
                                    xs = x1t[:, slot * NT:(slot + 1) * NT]
                                    v.tensor_tensor(
                                        xs, r1[:, c * TOK + th * NT: c * TOK + (th + 1) * NT],
                                        pbca[:, :], op=ALU.mult)
                                    ins = v.tensor_tensor(xs, xs, pbcb[:, :], op=ALU.add)
                                    sV.inc(ins)
                                    norm1_done[k] = sV.val

                                @blk.scalar
                                def _(s, th=th, c=c, k=k):
                                    slot = k % 2
                                    s.wait_ge(sV.h, norm1_done[k])
                                    ins = s.activation(
                                        xb[:, c * TOK + th * NT: c * TOK + (th + 1) * NT],
                                        x1t[:, slot * NT:(slot + 1) * NT], AF.Identity)
                                    sS.inc(ins)
                                    x1_cast[k] = sS.val

                                @blk.sync
                                def _(y, th=th, c=c, k=k):
                                    slot = k % 2
                                    y.wait_ge(sV.h, norm1_done[k])
                                    y.dma_start(
                                        x1sp[c * 128:(c + 1) * 128, th * NT:(th + 1) * NT],
                                        x1t[:, slot * NT:(slot + 1) * NT]
                                    ).then_inc(sX1O.h, 16)
                                    sX1O.bump()

            if stage == "wv":
                dwv = nc.dram_tensor("dwv", [128, DC * TOK], BF16, kind="ExternalOutput").ap()
                sDBG = Ctr(nc, "sDBG", 16)

                @blk.sync
                def _(y):
                    y.wait_ge(sS.h, wv_ev[HEADS - 1])
                    y.dma_start(dwv[:, :], wvT[:, :]).then_inc(sDBG.h, 16)
                    y.wait_ge(sDBG.h, 16)

            if dbg:
                wvTd = nc.dram_tensor("wvTd", [128, DC * TOK], BF16,
                                      kind="ExternalOutput").ap()
                qtrd = nc.dram_tensor("qtrd", [128, 2 * TOK], BF16,
                                      kind="ExternalOutput").ap()
                r1d = nc.dram_tensor("r1d", [128, DC * TOK], F32,
                                     kind="ExternalOutput").ap()
                sDBG2 = Ctr(nc, "sDBG2", 16)

                @blk.sync
                def _(y):
                    y.wait_ge(sV.h, norm1_done[TH * DC - 1])
                    y.dma_start(wvTd[:, :], wvT[:, :]).then_inc(sDBG2.h, 16)
                    y.dma_start(qtrd[:, :], qtr[:, :]).then_inc(sDBG2.h, 16)
                    y.dma_start(r1d[:, :], r1[:, :]).then_inc(sDBG2.h, 16)
                    y.wait_ge(sDBG2.h, 48)

            @blk.gpsimd
            def _(g):
                for j, src in enumerate(qv_src):
                    if j >= 3:
                        g.wait_ge(sT.h, qv_last_use[j - 3])
                    g.dma_start(qv_slot(j), src.rearrange("p c o -> p (c o)")
                                ).then_inc(qvS[j % 4].h, 16)
                    qvS[j % 4].bump()
                j = len(qv_src)
                g.dma_start(latTs[:, 0:L], latT[:, 0:L]).then_inc(qvS[j % 4].h, 16)
                qvS[j % 4].bump()

        with ExitStack() as _ffn:
            ef = _ffn.enter_context
            hid = ef(nc.sbuf_tensor("hid", [128, HC * TOK], BF16))
            silut = ef(nc.sbuf_tensor("silut", [128, 2 * NT], F32))
            bsp = ef(nc.sbuf_tensor("bsp", [128, TOK], BF16))
            r2r = ef(nc.sbuf_tensor("r2r", [128, 4 * NT], F32))
            r2b = ef(nc.sbuf_tensor("r2b", [128, NT], BF16))
            sq2 = ef(nc.sbuf_tensor("sq2", [128, NT], BF16))
            def w13_slot(j):
                return w13w[:, (j % 4) * D:((j % 4) + 1) * D]

            def w2_slot(j):
                return w2w[:, (j % 3) * 1024:((j % 3) + 1) * 1024]

            w13_last_use = [None] * (2 * HC)
            w2_last_use = [None] * (2 * DC * 8)

            up_ev = [[None] * TH for _ in range(HC)]
            ug_done = [[None] * TH for _ in range(HC)]
            ul_done = [[None] * TH for _ in range(HC)]
            silu_done = [[None] * TH for _ in range(HC)]

            smap = [list(range(HC)) for _ in range(TH)]
            spare = [HC, HC + 1]

            def hslice(phys, th):
                if phys < HC:
                    return hid[:, phys * TOK + th * NT: phys * TOK + (th + 1) * NT]
                u = phys - HC
                return bsp[:, u * NT:(u + 1) * NT]

            bstage_done = [None, None]
            bf_final = [[0] * HC for _ in range(TH)]
            stage_state = [
                {"stage": 0, "idx": 0, "done": [set() for _ in range(len(BST) + 1)]}
                for _ in range(TH)]

            def emit_ready_butterflies(v, th):
                st = stage_state[th]
                while st["stage"] < len(BST):
                    s_i = st["stage"]
                    pairs = BST[s_i]
                    progressed = False
                    while st["idx"] < len(pairs):
                        a, b = pairs[st["idx"]]
                        if not (a in st["done"][s_i] and b in st["done"][s_i]):
                            break
                        pa, pb = smap[th][a], smap[th][b]
                        sp = spare[th]
                        v.tensor_tensor(hslice(sp, th), hslice(pa, th),
                                        hslice(pb, th), op=ALU.add)
                        ins = v.tensor_tensor(hslice(pb, th), hslice(pa, th),
                                              hslice(pb, th), op=ALU.subtract)
                        sV.inc(ins)
                        smap[th][a] = sp
                        spare[th] = pa
                        bf_final[th][a] = sV.val
                        bf_final[th][b] = sV.val
                        st["done"][s_i + 1].add(a)
                        st["done"][s_i + 1].add(b)
                        st["idx"] += 1
                        progressed = True
                    if st["idx"] == len(pairs):
                        st["stage"] += 1
                        st["idx"] = 0
                        if st["stage"] == len(BST):
                            bstage_done[th] = sV.val
                        continue
                    if not progressed:
                        break

            w2_done = [[None] * DC for _ in range(TH)]
            r2_add = [[None] * DC for _ in range(TH)]
            r2_cast = [[None] * DC for _ in range(TH)]
            sq2_done = [[None] * DC for _ in range(TH)]
            st2_sum = [[None] * DC for _ in range(TH)]
            st2_sq = [[None] * DC for _ in range(TH)]
            stageA_done = [[None] * HC for _ in range(TH)]

            with ExitStack() as _st2:
                pstat2A = _st2.enter_context(nc.psum_tensor("pstat2A", [1, TOK], F32))
                pstat2B = _st2.enter_context(nc.psum_tensor("pstat2B", [1, TOK], F32))
                with ExitStack() as _up:
                    pug = _up.enter_context(nc.psum_tensor("pug", [128, TOK], F32))
                    pul = _up.enter_context(nc.psum_tensor("pul", [128, TOK], F32))
                    for hc in range(HC):
                        @blk.tensor
                        def _(t, hc=hc):
                            j1, j3 = 2 * hc, 2 * hc + 1
                            t.wait_ge(w13S[(j1 + 1) % 4].h, 16 * ((j1 + 1) // 4 + 1))
                            if hc >= 1:
                                t.wait_ge(sV.h, up_ev[hc - 1][TH - 1])
                            for th in range(TH):
                                for kc in range(DC):
                                    if hc == 0:
                                        t.wait_ge(sS.h, x1_cast[th * DC + kc])
                                    ins = t.matmul(
                                        pug[:, th * NT:(th + 1) * NT],
                                        w13_slot(j1)[:, kc * 128:(kc + 1) * 128],
                                        xb[:, kc * TOK + th * NT: kc * TOK + (th + 1) * NT],
                                        start=(kc == 0), stop=(kc == DC - 1))
                                sT.inc(ins)
                                ug_done[hc][th] = sT.val
                            w13_last_use[j1] = sT.val
                            t.wait_ge(w13S[(j3 + 1) % 4].h, 16 * ((j3 + 1) // 4 + 1))
                            for th in range(TH):
                                for kc in range(DC):
                                    ins = t.matmul(
                                        pul[:, th * NT:(th + 1) * NT],
                                        w13_slot(j3)[:, kc * 128:(kc + 1) * 128],
                                        xb[:, kc * TOK + th * NT: kc * TOK + (th + 1) * NT],
                                        start=(kc == 0), stop=(kc == DC - 1))
                                sT.inc(ins)
                                ul_done[hc][th] = sT.val
                            w13_last_use[j3] = sT.val

                        @blk.scalar
                        def _(s, hc=hc):
                            for th in range(TH):
                                k = hc * TH + th
                                slot = k % 2
                                s.wait_ge(sT.h, ug_done[hc][th])
                                if k >= 2:
                                    pr = k - 2
                                    s.wait_ge(sV.h, up_ev[pr // TH][pr % TH])
                                ins = s.activation(silut[:, slot * NT:(slot + 1) * NT],
                                                   pug[:, th * NT:(th + 1) * NT], AF.Silu)
                                sS.inc(ins)
                                silu_done[hc][th] = sS.val

                        @blk.vector
                        def _(v, hc=hc):
                            for th in range(TH):
                                slot = (hc * TH + th) % 2
                                v.wait_ge(sS.h, silu_done[hc][th])
                                v.wait_ge(sT.h, ul_done[hc][th])
                                ins = v.tensor_tensor(
                                    hid[:, hc * TOK + th * NT: hc * TOK + (th + 1) * NT],
                                    silut[:, slot * NT:(slot + 1) * NT],
                                    pul[:, th * NT:(th + 1) * NT], op=ALU.mult)
                                sV.inc(ins)
                                up_ev[hc][th] = sV.val
                                stage_state[th]["done"][0].add(hc)
                                emit_ready_butterflies(v, th)

                with ExitStack() as _aw:
                    pa = _aw.enter_context(nc.psum_tensor("pa", [128, TOK], F32))
                    pw2 = _aw.enter_context(nc.psum_tensor("pw2", [128, TOK], F32))
                    for th in range(TH):
                        for blkk in range(8):
                            for q in range(8):
                                h1 = blkk * 8 + q

                                @blk.tensor
                                def _(t, th=th, h1=h1):
                                    phys = smap[th][h1]
                                    if h1 == 0 and th == 0:
                                        t.wait_ge(sH128.h, 16)
                                    t.wait_ge(sV.h, bf_final[th][h1])
                                    if h1 >= 2:
                                        t.wait_ge(sS.h, stageA_done[th][h1 - 2])
                                    ins = t.matmul(
                                        pa[:, (h1 % 2) * NT:((h1 % 2) + 1) * NT],
                                        h128s[:, :], hslice(phys, th),
                                        start=True, stop=True)
                                    sT.inc(ins)

                                @blk.scalar
                                def _(s, th=th, h1=h1):
                                    phys = smap[th][h1]
                                    s.wait_ge(sT.h, sT.val)
                                    ins = s.activation(
                                        hslice(phys, th),
                                        pa[:, (h1 % 2) * NT:((h1 % 2) + 1) * NT],
                                        AF.Identity, scale=RSQ_HID)
                                    sS.inc(ins)
                                    stageA_done[th][h1] = sS.val

                            @blk.tensor
                            def _(t, th=th, blkk=blkk):
                                if blkk == 0 and th >= 1:
                                    t.wait_ge(sV.h, r2_add[th - 1][DC - 1])
                                j = th * DC * 8 + blkk
                                t.wait_ge(w2S[(j + 1) % 3].h, 16 * ((j + 1) // 3 + 1))
                                t.wait_ge(sS.h, stageA_done[th][blkk * 8 + 7])
                                for q in range(8):
                                    h1 = blkk * 8 + q
                                    ins = t.matmul(
                                        pw2[:, th * NT:(th + 1) * NT],
                                        w2_slot(j)[:, q * 128:(q + 1) * 128],
                                        hslice(smap[th][h1], th),
                                        start=(h1 == 0), stop=(h1 == HC - 1))
                                sT.inc(ins)
                                w2_last_use[j] = sT.val
                                if blkk == 7:
                                    w2_done[th][0] = sT.val
                                    k = th * DC
                                    if k >= 1:
                                        emit_stats(t, (k - 1) // DC, (k - 1) % DC)

                        def emit_stats(t, pth, pdc):
                            t.wait_ge(sS.h, r2_cast[pth][pdc])
                            ins = t.matmul(pstat2A[0:1, pth * NT:(pth + 1) * NT],
                                           onesb[:, :],
                                           r2b[:, :],
                                           start=(pdc == 0), stop=(pdc == DC - 1))
                            sT.inc(ins)
                            st2_sum[pth][pdc] = sT.val
                            t.wait_ge(sV.h, sq2_done[pth][pdc])
                            ins = t.matmul(pstat2B[0:1, pth * NT:(pth + 1) * NT],
                                           onesb[:, :],
                                           sq2[:, :],
                                           start=(pdc == 0), stop=(pdc == DC - 1))
                            sT.inc(ins)
                            st2_sq[pth][pdc] = sT.val

                        for dc in range(DC):
                            if dc >= 1:
                                @blk.tensor
                                def _(t, th=th, dc=dc):
                                    t.wait_ge(sV.h, r2_add[th][dc - 1])
                                    for blkk in range(8):
                                        j = th * DC * 8 + dc * 8 + blkk
                                        t.wait_ge(w2S[(j + 1) % 3].h, 16 * ((j + 1) // 3 + 1))
                                        for q in range(8):
                                            h1 = blkk * 8 + q
                                            ins = t.matmul(
                                                pw2[:, th * NT:(th + 1) * NT],
                                                w2_slot(j)[:, q * 128:(q + 1) * 128],
                                                hslice(smap[th][h1], th),
                                                start=(h1 == 0), stop=(h1 == HC - 1))
                                        sT.inc(ins)
                                        w2_last_use[j] = sT.val
                                    w2_done[th][dc] = sT.val
                                    k = th * DC + dc
                                    emit_stats(t, (k - 1) // DC, (k - 1) % DC)

                            @blk.sync
                            def _(y, th=th, dc=dc):
                                k = th * DC + dc
                                u = k % 4
                                if k == 0:
                                    y.wait_ge(sX1O.h, 16 * TH * DC)
                                    y.wait_ge(sT.h, last_attn_T[0])
                                    y.wait_ge(sV.h, norm1_done[TH * DC - 1])
                                if k >= 4:
                                    pr = k - 4
                                    y.wait_ge(sS.h, r2_cast[pr // DC][pr % DC])
                                    y.wait_ge(sR2O.h, 16 * (pr + 1))
                                y.dma_start(
                                    r2r[:, u * NT:(u + 1) * NT],
                                    x1sp[dc * 128:(dc + 1) * 128, th * NT:(th + 1) * NT]
                                ).then_inc(x1iS[u].h, 16)
                                x1iS[u].bump()

                            @blk.vector
                            def _(v, th=th, dc=dc):
                                u = (th * DC + dc) % 4
                                v.wait_ge(x1iS[u].h, x1iS[u].val)
                                v.wait_ge(sT.h, w2_done[th][dc])
                                sl = r2r[:, u * NT:(u + 1) * NT]
                                ins = v.tensor_tensor(sl, sl,
                                                      pw2[:, th * NT:(th + 1) * NT],
                                                      op=ALU.add)
                                sV.inc(ins)
                                r2_add[th][dc] = sV.val

                            @blk.scalar
                            def _(s, th=th, dc=dc):
                                k = th * DC + dc
                                u = k % 4
                                s.wait_ge(sV.h, r2_add[th][dc])
                                if k >= 1:
                                    pr = k - 1
                                    s.wait_ge(sT.h, st2_sum[pr // DC][pr % DC])
                                    s.wait_ge(sV.h, sq2_done[pr // DC][pr % DC])
                                ins = s.activation(r2b[:, :],
                                                   r2r[:, u * NT:(u + 1) * NT],
                                                   AF.Identity)
                                sS.inc(ins)
                                r2_cast[th][dc] = sS.val

                            @blk.vector
                            def _(v, th=th, dc=dc):
                                k = th * DC + dc
                                v.wait_ge(sS.h, r2_cast[th][dc])
                                if k >= 1:
                                    pr = k - 1
                                    v.wait_ge(sT.h, st2_sq[pr // DC][pr % DC])
                                rb = r2b[:, :]
                                ins = v.tensor_tensor(sq2[:, :],
                                                      rb, rb, op=ALU.mult)
                                sV.inc(ins)
                                sq2_done[th][dc] = sV.val

                            @blk.sync
                            def _(y, th=th, dc=dc):
                                u = (th * DC + dc) % 4
                                y.wait_ge(sS.h, r2_cast[th][dc])
                                y.dma_start(
                                    r2sp[dc * 128:(dc + 1) * 128, th * NT:(th + 1) * NT],
                                    r2r[:, u * NT:(u + 1) * NT]).then_inc(sR2O.h, 16)
                                sR2O.bump()

                    @blk.tensor
                    def _(t):
                        emit_stats(t, TH - 1, DC - 1)

                ln2 = emit_ln_stats(ab, r2r[0:1, 0:2 * TOK], pstat2A, pstat2B, sV, sS)

                @blk.vector
                def _(v):
                    v.wait_ge(sT.h, st2_sq[TH - 1][DC - 1])
                    ln2[0](v)

                @blk.scalar
                def _(s):
                    s.wait_ge(sV.h, sV.val)
                    ln2[1](s)

                @blk.vector
                def _(v):
                    v.wait_ge(sS.h, sS.val)
                    ln2[2](v)
                ln2_ab = sV.val

                fin_done = [None] * DC
                with ExitStack() as _bc2:
                    pbca2 = _bc2.enter_context(nc.psum_tensor("pbca2", [128, TOK], F32))
                    pbcb2 = _bc2.enter_context(nc.psum_tensor("pbcb2", [128, TOK], F32))
                    @blk.tensor
                    def _(t):
                        t.wait_ge(sV.h, ln2_ab)
                        for th in range(TH):
                            t.matmul(pbca2[:, th * NT:(th + 1) * NT], onesf[0:1, :],
                                     ab[0:1, th * NT:(th + 1) * NT],
                                     start=True, stop=True)
                            ins = t.matmul(pbcb2[:, th * NT:(th + 1) * NT],
                                           onesf[0:1, :],
                                           ab[0:1, TOK + th * NT: TOK + (th + 1) * NT],
                                           start=True, stop=True)
                        sT.inc(ins)
                    bc2 = sT.val

                    r2in_val = [None] * DC
                    for c in range(DC):
                        @blk.sync
                        def _(y, c=c):
                            u = c % 2
                            if c == 0:
                                y.wait_ge(sR2O.h, 16 * TH * DC)
                                y.wait_ge(sV.h, ln2_ab)
                            if c >= 2:
                                y.wait_ge(sV.h, fin_done[c - 2])
                                y.wait_ge(sOUT.h, 16 * (c - 1))
                            y.dma_start(r2r[:, u * TOK:(u + 1) * TOK],
                                        r2sp[c * 128:(c + 1) * 128, :]
                                        ).then_inc(r2iS[u].h, 16)
                            r2iS[u].bump()
                            r2in_val[c] = r2iS[u].val

                        @blk.vector
                        def _(v, c=c):
                            u = c % 2
                            if c == 0:
                                v.wait_ge(sT.h, bc2)
                            v.wait_ge(r2iS[u].h, r2in_val[c])
                            fs = r2r[:, u * TOK:(u + 1) * TOK]
                            v.tensor_tensor(fs, fs, pbca2[:, :], op=ALU.mult)
                            ins = v.tensor_tensor(fs, fs, pbcb2[:, :], op=ALU.add)
                            sV.inc(ins)
                            fin_done[c] = sV.val

                        @blk.sync
                        def _(y, c=c):
                            u = c % 2
                            y.wait_ge(sV.h, fin_done[c])
                            y.dma_start(outT[c * 128:(c + 1) * 128, :],
                                        r2r[:, u * TOK:(u + 1) * TOK]
                                        ).then_inc(sOUT.h, 16)
                            sOUT.bump()

                    @blk.sync
                    def _(y):
                        y.wait_ge(sOUT.h, 16 * DC)

            @blk.gpsimd
            def _(g):
                for j in range(2 * HC):
                    hc, is3 = j // 2, j % 2
                    src = (w3 if is3 else w1)[hc]
                    if j >= 4:
                        g.wait_ge(sT.h, w13_last_use[j - 4])
                    g.dma_start(w13_slot(j), src.rearrange("p c o -> p (c o)")
                                ).then_inc(w13S[j % 4].h, 16)
                    w13S[j % 4].bump()
                j = 2 * HC
                g.dma_start(h128s[:, :], h128h[:, :]).then_inc(w13S[j % 4].h, 16)
                w13S[j % 4].bump()
                for j in range(2 * DC * 8):
                    th = j // (DC * 8)
                    dc = (j // 8) % DC
                    blkk = j % 8
                    src = w2[dc, :, blkk * 8:(blkk + 1) * 8, :]
                    if j >= 3:
                        g.wait_ge(sT.h, w2_last_use[j - 3])
                    g.dma_start(w2_slot(j), src.rearrange("p c o -> p (c o)")
                                ).then_inc(w2S[j % 3].h, 16)
                    w2S[j % 3].bump()
                j = 2 * DC * 8
                g.dma_start(h128s[:, :], h128h[:, :]).then_inc(w2S[j % 3].h, 16)
                w2S[j % 3].bump()

    return nc


def _prep_w(w, ocn, kcn):
    return np.ascontiguousarray(
        w.reshape(ocn, 128, kcn, 128).transpose(0, 3, 2, 1)).astype(np.float32)


def kernel(**inputs):
    x = np.asarray(inputs["L0_attn_hidden_states"], dtype=np.float32)
    Wq = np.asarray(inputs["L0_attn_Wq"], dtype=np.float32)
    bq = np.asarray(inputs["L0_attn_bq"], dtype=np.float32)
    Wv = np.asarray(inputs["L0_attn_Wv"], dtype=np.float32)
    bv = np.asarray(inputs["L0_attn_bv"], dtype=np.float32)
    Wo = np.asarray(inputs["L0_attn_Wo"], dtype=np.float32)
    bo = np.asarray(inputs["L0_attn_bo"], dtype=np.float32)
    lat = np.asarray(inputs["L0_attn_latent_keys"], dtype=np.float32)
    W1 = np.asarray(inputs["L0_ffn_W1"], dtype=np.float32)
    W2 = np.asarray(inputs["L0_ffn_W2"], dtype=np.float32)
    W3 = np.asarray(inputs["L0_ffn_W3"], dtype=np.float32)

    X = x.reshape(NCORES * TOK, D)
    wq_t = _prep_w(Wq, DC, DC)
    wv_t = _prep_w(Wv, DC, DC)
    wo_t = _prep_w(Wo, DC, DC)
    w1_t = _prep_w(W1, HC, DC)
    w3_t = _prep_w(W3, HC, DC)
    w2_t = _prep_w(W2, DC, HC)
    latT = np.ascontiguousarray(
        lat.reshape(L, DC, 128).transpose(2, 1, 0).reshape(128, DC * L)).astype(np.float32)
    bia = np.ascontiguousarray(
        np.stack([bq, bv, bo], 0).reshape(3, DC, 128).transpose(2, 0, 1)
        .reshape(128, 48)).astype(np.float32)
    h128 = hadamard(128).astype(bfdt)
    ident = np.eye(128, dtype=bfdt)
    onesb = np.ones((128, 1), dtype=bfdt)
    onesf = np.ones((1, 128), dtype=np.float32)

    nc = build_nc()
    in_maps = []
    for i in range(NCORES):
        xi = np.ascontiguousarray(X[i * TOK:(i + 1) * TOK].T)
        in_maps.append({
            "xT": xi, "wq": wq_t, "wv": wv_t, "wo": wo_t,
            "w1": w1_t, "w3": w3_t, "w2": w2_t,
            "latT": latT, "latN": lat, "bia": bia,
            "h128c": h128, "identc": ident, "onesbc": onesb, "onesfc": onesf,
        })
    res = run_bass_kernel_spmd(nc, in_maps, core_ids=list(range(NCORES)))
    outs = [np.asarray(r["outT"]).T for r in res.results]
    return np.concatenate(outs, axis=0).reshape(2, 4096, D).astype(np.float32)


# version 19 (final)
# speedup vs baseline: 1.0757x; 1.0757x over previous
import math
from contextlib import ExitStack
import numpy as np
import ml_dtypes

import concourse.bass as bass
import concourse.mybir as mybir
from concourse.bass_utils import run_bass_kernel_spmd

F32 = mybir.dt.float32
BF16 = mybir.dt.bfloat16
AF = mybir.ActivationFunctionType
ALU = mybir.AluOpType
AX = mybir.AxisListType
bfdt = ml_dtypes.bfloat16

NCORES = 8
TOK = 1024
TC = TOK // 128
TH = 2
NT = TOK // TH
D = 2048
DC = D // 128
HEADS = 16
L = 64
HID = 8192
HC = HID // 128

RSQ_HD = 1.0 / math.sqrt(128.0)
RSQ_HID = 1.0 / math.sqrt(8192.0)
EPS = 1e-5
NEG = -1e30
QV_SERIAL = 4


class Ctr:
    def __init__(self, nc, name, step=1):
        self.h = nc.semaphore(name).__enter__()
        self.step = step
        self.n = 0

    def inc(self, instr):
        instr.then_inc(self.h, self.step)
        self.n += 1
        return self.n * self.step

    def bump(self):
        self.n += 1
        return self.n * self.step

    @property
    def val(self):
        return self.n * self.step


def butterfly_stages():
    out = []
    h = 1
    while h < HC:
        stage = []
        for g in range(0, HC, 2 * h):
            for j in range(h):
                stage.append((g + j, g + h + j))
        out.append(stage)
        h *= 2
    return out


def hadamard(n):
    h = np.array([[1.0]])
    while h.shape[0] < n:
        h = np.block([[h, h], [h, -h]])
    return h


def emit_ln_stats(ab, scr, pstatA, pstatB, sV, sS):
    A = ab[0:1, 0:TOK]
    B = ab[0:1, TOK:2 * TOK]
    M = scr[0:1, 0:TOK]
    E = scr[0:1, TOK:2 * TOK]

    def f1(v):
        v.tensor_scalar(M, pstatA[0:1, :], 1.0 / D, None, op0=ALU.mult)
        v.tensor_scalar(E, pstatB[0:1, :], 1.0 / D, None, op0=ALU.mult)
        v.tensor_tensor(B, M, M, op=ALU.mult)
        v.tensor_tensor(E, E, B, op=ALU.subtract)
        ins = v.tensor_scalar(E, E, EPS, None, op0=ALU.add)
        return sV.inc(ins)

    def f2(s):
        ins = s.activation(B, E, AF.Sqrt)
        return sS.inc(ins)

    def f3(v):
        ins = v.reciprocal(A, B)
        sV.inc(ins)
        v.wait_ge(sV.h, sV.val)
        v.tensor_tensor(B, A, A, op=ALU.mult)
        v.tensor_tensor(B, B, E, op=ALU.mult)
        v.tensor_scalar(B, B, -0.5, 1.5, op0=ALU.mult, op1=ALU.add)
        v.tensor_tensor(A, A, B, op=ALU.mult)
        v.tensor_tensor(B, M, A, op=ALU.mult)
        ins = v.tensor_scalar(B, B, -1.0, None, op0=ALU.mult)
        return sV.inc(ins)

    return f1, f2, f3


def build_nc(dbg=False, stage=None):
    nc = bass.Bass("TRN2", target_bir_lowering=False, debug=False,
                   detect_race_conditions=False)

    xT = nc.dram_tensor("xT", [D, TOK], F32, kind="ExternalInput").ap()
    wq = nc.dram_tensor("wq", [DC, 128, DC, 128], F32, kind="ExternalInput").ap()
    wv = nc.dram_tensor("wv", [DC, 128, DC, 128], F32, kind="ExternalInput").ap()
    wo = nc.dram_tensor("wo", [DC, 128, DC, 128], F32, kind="ExternalInput").ap()
    w1 = nc.dram_tensor("w1", [HC, 128, DC, 128], F32, kind="ExternalInput").ap()
    w3 = nc.dram_tensor("w3", [HC, 128, DC, 128], F32, kind="ExternalInput").ap()
    w2 = nc.dram_tensor("w2", [DC, 128, HC, 128], F32, kind="ExternalInput").ap()
    latT = nc.dram_tensor("latT", [128, DC * L], F32, kind="ExternalInput").ap()
    latN = nc.dram_tensor("latN", [L, D], F32, kind="ExternalInput").ap()
    bia = nc.dram_tensor("bia", [128, 48], F32, kind="ExternalInput").ap()
    h128h = nc.dram_tensor("h128c", [128, 128], BF16, kind="ExternalInput").ap()
    identh = nc.dram_tensor("identc", [128, 128], BF16, kind="ExternalInput").ap()
    onesbh = nc.dram_tensor("onesbc", [128, 1], BF16, kind="ExternalInput").ap()
    onesfh = nc.dram_tensor("onesfc", [1, 128], F32, kind="ExternalInput").ap()
    spk = "ExternalOutput" if dbg else "Internal"
    x1sp = nc.dram_tensor("x1sp", [D, TOK], F32, kind=spk).ap()
    r2sp = nc.dram_tensor("r2sp", [D, TOK], F32, kind=spk).ap()
    outT = nc.dram_tensor("outT", [D, TOK], F32, kind="ExternalOutput").ap()

    BST = butterfly_stages()

    with ExitStack() as _root:
        ec = _root.enter_context
        xb = ec(nc.sbuf_tensor("xb", [128, DC * TOK], BF16))
        onesb = ec(nc.sbuf_tensor("onesb", [128, 1], BF16))
        onesf = ec(nc.sbuf_tensor("onesf", [1, 128], F32))
        ab = ec(nc.sbuf_tensor("ab", [1, 2 * TOK], F32))
        w13w = ec(nc.sbuf_tensor("w13w", [128, 4 * D], BF16))
        w2w = ec(nc.sbuf_tensor("w2w", [128, 3 * 1024], BF16))
        h128s = ec(nc.sbuf_tensor("h128s", [128, 128], BF16))
        blk = ec(nc.Block())
        sT = Ctr(nc, "sT")
        sV = Ctr(nc, "sV")
        sS = Ctr(nc, "sS")
        sPRO = Ctr(nc, "sPRO", 16)
        sYP = Ctr(nc, "sYP", 16)
        qvS = [Ctr(nc, f"qv{i}", 16) for i in range(4)]
        w13S = [Ctr(nc, f"w13{i}", 16) for i in range(4)]
        w2S = [Ctr(nc, f"w2{i}", 16) for i in range(3)]
        sX1O = Ctr(nc, "sX1O", 16)
        sR2O = Ctr(nc, "sR2O", 16)
        sH128 = Ctr(nc, "sH128", 16)
        x1iS = [Ctr(nc, f"x1i{i}", 16) for i in range(4)]
        r2iS = [Ctr(nc, f"r2i{i}", 16) for i in range(2)]
        sOUT = Ctr(nc, "sOUT", 16)

        x1_cast = [None] * (TH * DC)
        norm1_done = [None] * (TH * DC)
        last_attn_T = [None]

        with ExitStack() as _attn:
            ea = _attn.enter_context
            wvT = ea(nc.sbuf_tensor("wvT", [128, DC * TOK], BF16))
            r1 = ea(nc.sbuf_tensor("r1", [128, DC * TOK], F32))
            qtr = ea(nc.sbuf_tensor("qtr", [128, 2 * TOK], BF16))
            lgs = ea(nc.sbuf_tensor("lgs", [128, TC * L], F32))
            msk = ea(nc.sbuf_tensor("msk", [128, TC * L], F32))
            m8s = ea(nc.sbuf_tensor("m8s", [128, TC * 8], F32))
            den = ea(nc.sbuf_tensor("den", [128, TC], F32))
            den2 = ea(nc.sbuf_tensor("den2", [128, TC], F32))
            wsm = ea(nc.sbuf_tensor("wsm", [128, TC * L], BF16))
            wTr = ea(nc.sbuf_tensor("wTr", [L, 2 * TOK], BF16))
            qvw = ea(nc.sbuf_tensor("qvw", [128, 3 * D], BF16))
            rbt = ea(nc.sbuf_tensor("rbt", [128, NT], BF16))
            sqt = ea(nc.sbuf_tensor("sqt", [128, NT], BF16))
            x1t = ea(nc.sbuf_tensor("x1t", [128, TOK], F32))
            lnscr = ea(nc.sbuf_tensor("lnscr", [1, 2 * TOK], F32))
            latTs = ea(nc.sbuf_tensor("latTs", [128, DC * L], BF16))
            latns = ea(nc.sbuf_tensor("latns", [L, D], BF16))
            idents = ea(nc.sbuf_tensor("idents", [128, 128], BF16))
            biass = ea(nc.sbuf_tensor("biass", [128, 48], F32))
            bqs = ea(nc.sbuf_tensor("bqs", [128, 16], F32))
            @blk.gpsimd
            def _(g):
                g.dma_start(latTs[:, :], latT[:, :]).then_inc(sPRO.h, 16); sPRO.bump()
                g.dma_start(latns[:, :], latN[:, :]).then_inc(sPRO.h, 16); sPRO.bump()
                for c in range(DC):
                    g.dma_start(xb[:, c * TOK:(c + 1) * TOK],
                                xT[c * 128:(c + 1) * 128, :]).then_inc(sPRO.h, 16)
                    sPRO.bump()
                g.dma_start(xb[:, 0:TOK],
                            xT[0:128, :]).then_inc(sPRO.h, 16)
                sPRO.bump()
            pro_all = sPRO.val

            @blk.sync
            def _(y):
                y.dma_start(idents[:, :], identh[:, :]).then_inc(sYP.h, 16); sYP.bump()
                y.dma_start(h128s[:, :], h128h[:, :]).then_inc(sH128.h, 16); sH128.bump()
                y.dma_start(biass[:, :], bia[:, :]).then_inc(sYP.h, 16); sYP.bump()
                y.dma_start(onesb[:, :], onesbh[:, :]).then_inc(sYP.h, 16); sYP.bump()
                y.dma_start(onesf[:, :], onesfh[:, :]).then_inc(sYP.h, 16); sYP.bump()
                for c in range(DC):
                    y.dma_start(r1[:, c * TOK:(c + 1) * TOK],
                                xT[c * 128:(c + 1) * 128, :]).then_inc(sYP.h, 16)
                    sYP.bump()
            yp_all = sYP.val

            @blk.vector
            def _(v):
                v.wait_ge(sYP.h, 16 * 2)
                ins = v.tensor_scalar(bqs[:, :], biass[:, 0:16], RSQ_HD, None, op0=ALU.mult)
                sV.inc(ins)
            bqs_ready = sV.val

            qv_src = []
            for h in range(HEADS):
                qv_src.append(wq[h])
                qv_src.append(wv[h])
            for oc in range(DC):
                qv_src.append(wo[oc])
            qv_last_use = [None] * len(qv_src)

            def qv_slot(j):
                return qvw[:, (j % 3) * D:((j % 3) + 1) * D]

            qt_ev = [None] * HEADS
            qt_ev_th = [[None] * TH for _ in range(HEADS)]
            q_done_th = [[None] * TH for _ in range(HEADS)]
            lg_done = [None] * HEADS
            lg_copy = [None] * HEADS
            top4_done = [None] * HEADS
            wt_ev = [None] * HEADS
            wv_ev = [None] * HEADS
            q_done = [None] * HEADS
            wtr_done = [None] * HEADS
            wgt_done = [None] * HEADS

            o_done = [[None] * TH for _ in range(DC)]
            r1_add = [[None] * TH for _ in range(DC)]
            r1_cast = [[None] * TH for _ in range(DC)]
            sq_done = [[None] * TH for _ in range(DC)]
            st_sum = [[None] * TH for _ in range(DC)]
            st_sq = [[None] * TH for _ in range(DC)]

            with nc.psum_tensor("pqo", [128, TOK], F32) as pqo:
                with ExitStack() as _hp:
                    pv = _hp.enter_context(nc.psum_tensor("pv", [128, 2 * TOK], F32))
                    plg = _hp.enter_context(nc.psum_tensor("plg", [128, 512], F32))
                    pwt = _hp.enter_context(nc.psum_tensor("pwt", [L, TOK], BF16))
                    def emit_q(t, h):
                        j = 2 * h
                        t.wait_ge(qvS[(j + 1) % 4].h, 16 * ((j + 1) // 4 + 1))
                        for th in range(TH):
                            if h >= 1:
                                t.wait_ge(sS.h, qt_ev_th[h - 1][th])
                            for kc in range(DC):
                                ins = t.matmul(
                                    pqo[:, th * NT:(th + 1) * NT],
                                    qv_slot(j)[:, kc * 128:(kc + 1) * 128],
                                    xb[:, kc * TOK + th * NT: kc * TOK + (th + 1) * NT],
                                    start=(kc == 0), stop=(kc == DC - 1))
                            sT.inc(ins)
                            q_done_th[h][th] = sT.val
                        q_done[h] = sT.val
                        qv_last_use[j] = sT.val

                    def emit_qt_evac(s, h):
                        if h == 0:
                            s.wait_ge(sV.h, bqs_ready)
                        if h >= 2:
                            s.wait_ge(sT.h, lg_done[h - 2])
                        qs = qtr[:, (h % 2) * TOK:((h % 2) + 1) * TOK]
                        for th in range(TH):
                            s.wait_ge(sT.h, q_done_th[h][th])
                            ins = s.activation(qs[:, th * NT:(th + 1) * NT],
                                               pqo[:, th * NT:(th + 1) * NT],
                                               AF.Identity, bias=bqs[:, h:h + 1],
                                               scale=RSQ_HD)
                            sS.inc(ins)
                            qt_ev_th[h][th] = sS.val
                        qt_ev[h] = sS.val

                    def emit_logits(t, h):
                        t.wait_ge(sS.h, qt_ev[h])
                        if h >= 1:
                            t.wait_ge(sV.h, lg_copy[h - 1])
                        qs = qtr[:, (h % 2) * TOK:((h % 2) + 1) * TOK]
                        for tc in range(TC):
                            ins = t.matmul(plg[:, tc * L:(tc + 1) * L],
                                           qs[:, tc * 128:(tc + 1) * 128],
                                           latTs[:, h * L:(h + 1) * L],
                                           start=True, stop=True)
                        sT.inc(ins)
                        lg_done[h] = sT.val

                    def emit_v(t, h):
                        j = 2 * h + 1
                        t.wait_ge(qvS[(j + 1) % 4].h, 16 * ((j + 1) // 4 + 1))
                        if h >= 2:
                            t.wait_ge(sS.h, wv_ev[h - 2])
                        pvb = (h % 2) * TOK
                        for th in range(TH):
                            for kc in range(DC):
                                ins = t.matmul(
                                    pv[:, pvb + th * NT: pvb + (th + 1) * NT],
                                    qv_slot(j)[:, kc * 128:(kc + 1) * 128],
                                    xb[:, kc * TOK + th * NT: kc * TOK + (th + 1) * NT],
                                    start=(kc == 0), stop=False)
                            sT.inc(ins)
                        qv_last_use[j] = sT.val

                    def emit_top4_pre(v, h):
                        v.wait_ge(sT.h, lg_done[h])
                        ins = v.tensor_copy(lgs[:, :], plg[:, :])
                        sV.inc(ins)
                        lg_copy[h] = sV.val
                        lg3 = lgs.rearrange("p (c l) -> p c l", l=L)
                        mk3 = msk.rearrange("p (c l) -> p c l", l=L)
                        for tc in range(TC):
                            v.max(m8s[:, tc * 8:(tc + 1) * 8], lg3[:, tc, :])
                        for tc in range(TC):
                            v.memset(m8s[:, tc * 8 + 4:tc * 8 + 8], NEG)
                        for tc in range(TC):
                            v.match_replace(mk3[:, tc, :], m8s[:, tc * 8:(tc + 1) * 8],
                                            lg3[:, tc, :], NEG)
                        m1 = m8s.rearrange("p (c k) -> p c k", k=8)[:, :, 0:1]
                        ins = v.tensor_tensor(lg3, lg3,
                                              m1.to_broadcast([128, TC, L]),
                                              op=ALU.subtract)
                        return sV.inc(ins)

                    def emit_exp(s, sub_val):
                        s.wait_ge(sV.h, sub_val)
                        ins = s.activation(lgs[:, :], lgs[:, :], AF.Exp)
                        sS.inc(ins)
                        return sS.val

                    def emit_top4_post(v, h, exp_val):
                        v.wait_ge(sS.h, exp_val)
                        if h >= 1:
                            v.wait_ge(sT.h, wtr_done[h - 1])
                        lg3 = lgs.rearrange("p (c l) -> p c l", l=L)
                        v.tensor_scalar(msk[:, :], msk[:, :], NEG, None, op0=ALU.is_equal)
                        ins = v.tensor_tensor(lgs[:, :], lgs[:, :], msk[:, :], op=ALU.mult)
                        sV.inc(ins)
                        v.wait_ge(sV.h, sV.val)
                        ins = v.tensor_reduce(den[:, :], lg3, axis=AX.X, op=ALU.add)
                        sV.inc(ins)
                        v.wait_ge(sV.h, sV.val)
                        ins = v.reciprocal(den2[:, :], den[:, :])
                        sV.inc(ins)
                        v.wait_ge(sV.h, sV.val)
                        d3 = den2.rearrange("p (c o) -> p c o", o=1)
                        ins = v.tensor_tensor(
                            wsm.rearrange("p (c l) -> p c l", l=L),
                            lg3, d3.to_broadcast([128, TC, L]), op=ALU.mult)
                        sV.inc(ins)
                        top4_done[h] = sV.val

                    def emit_wtrans(t, h):
                        t.wait_ge(sV.h, top4_done[h])
                        if h >= 1:
                            t.wait_ge(sS.h, wt_ev[h - 1])
                        for tc in range(TC):
                            ins = t.transpose(pwt[:, tc * 128:(tc + 1) * 128],
                                              wsm[:, tc * L:(tc + 1) * L],
                                              idents[:, :])
                        sT.inc(ins)
                        wtr_done[h] = sT.val

                    def emit_wt_evac(s, h):
                        s.wait_ge(sT.h, wtr_done[h])
                        if h >= 2:
                            s.wait_ge(sT.h, wgt_done[h - 2])
                        ws = wTr[:, (h % 2) * TOK:((h % 2) + 1) * TOK]
                        ins = s.activation(ws[:, :], pwt[:, :], AF.Identity)
                        sS.inc(ins)
                        wt_ev[h] = sS.val

                    def emit_weighted(t, h):
                        t.wait_ge(sS.h, wt_ev[h])
                        ws = wTr[:, (h % 2) * TOK:((h % 2) + 1) * TOK]
                        pvb = (h % 2) * TOK
                        for th in range(TH):
                            ins = t.matmul(pv[:, pvb + th * NT: pvb + (th + 1) * NT],
                                           latns[:, h * 128:(h + 1) * 128],
                                           ws[:, th * NT:(th + 1) * NT],
                                           start=False, stop=True)
                        sT.inc(ins)
                        wgt_done[h] = sT.val

                    def emit_wv_evac(s, h):
                        s.wait_ge(sT.h, wgt_done[h])
                        pvb = (h % 2) * TOK
                        for th in range(TH):
                            ins = s.activation(
                                wvT[:, h * TOK + th * NT: h * TOK + (th + 1) * NT],
                                pv[:, pvb + th * NT: pvb + (th + 1) * NT],
                                AF.Identity, bias=biass[:, 16 + h:17 + h])
                        sS.inc(ins)
                        wv_ev[h] = sS.val

                    @blk.tensor
                    def _(t):
                        t.wait_ge(sPRO.h, pro_all)
                        t.wait_ge(sYP.h, 16 * 1)
                        emit_q(t, 0)

                    @blk.scalar
                    def _(s):
                        emit_qt_evac(s, 0)

                    @blk.tensor
                    def _(t):
                        emit_logits(t, 0)

                    HR = 1 if stage == "h0" else HEADS
                    for h in range(HR):
                        sub_val = [None]

                        @blk.vector
                        def _(v, h=h):
                            sub_val[0] = emit_top4_pre(v, h)

                        @blk.tensor
                        def _(t, h=h):
                            emit_v(t, h)
                            if h + 1 < HR:
                                emit_q(t, h + 1)

                        exp_val = [None]

                        @blk.scalar
                        def _(s, h=h):
                            exp_val[0] = emit_exp(s, sub_val[0])
                            if h + 1 < HR:
                                emit_qt_evac(s, h + 1)
                            if h >= 1:
                                emit_wv_evac(s, h - 1)

                        @blk.vector
                        def _(v, h=h):
                            emit_top4_post(v, h, exp_val[0])

                        if h + 1 < HR:
                            @blk.tensor
                            def _(t, h=h):
                                emit_logits(t, h + 1)

                        @blk.tensor
                        def _(t, h=h):
                            emit_wtrans(t, h)

                        @blk.scalar
                        def _(s, h=h):
                            emit_wt_evac(s, h)

                        @blk.tensor
                        def _(t, h=h):
                            emit_weighted(t, h)

                    @blk.scalar
                    def _(s):
                        emit_wv_evac(s, HR - 1)

                if stage == "h0":
                    dq = nc.dram_tensor("dq", [128, TOK], BF16, kind="ExternalOutput").ap()
                    dlg = nc.dram_tensor("dlg", [128, TC * L], F32, kind="ExternalOutput").ap()
                    dws = nc.dram_tensor("dws", [128, TC * L], BF16, kind="ExternalOutput").ap()
                    dwt = nc.dram_tensor("dwt", [L, TOK], BF16, kind="ExternalOutput").ap()
                    dwv0 = nc.dram_tensor("dwv0", [128, TOK], BF16, kind="ExternalOutput").ap()
                    sDBG = Ctr(nc, "sDBG", 16)

                    @blk.sync
                    def _(y):
                        y.wait_ge(sS.h, wv_ev[0])
                        y.dma_start(dq[:, :], qtr[:, 0:TOK]).then_inc(sDBG.h, 16)
                        y.dma_start(dlg[:, :], lgs[:, :]).then_inc(sDBG.h, 16)
                        y.dma_start(dws[:, :], wsm[:, :]).then_inc(sDBG.h, 16)
                        y.dma_start(dwt[:, :], wTr[:, 0:TOK]).then_inc(sDBG.h, 16)
                        y.dma_start(dwv0[:, :], wvT[:, 0:TOK]).then_inc(sDBG.h, 16)
                        y.wait_ge(sDBG.h, 80)

                    @blk.gpsimd
                    def _(g):
                        for j, src_ in enumerate(qv_src[:2]):
                            g.dma_start(qv_slot(j), src_.rearrange("p c o -> p (c o)")
                                        ).then_inc(qvS[j % 4].h, 16)
                            qvS[j % 4].bump()
                    return nc

                with ExitStack() as _st:
                    pstatA = _st.enter_context(nc.psum_tensor("pstatA", [1, TOK], F32))
                    pstatB = _st.enter_context(nc.psum_tensor("pstatB", [1, TOK], F32))

                    for oc in range(DC):
                        @blk.tensor
                        def _(t, oc=oc):
                            j = 32 + oc
                            t.wait_ge(qvS[(j + 1) % 4].h, 16 * ((j + 1) // 4 + 1))
                            if oc == 0:
                                t.wait_ge(sS.h, wv_ev[HEADS - 1])
                            else:
                                t.wait_ge(sV.h, r1_add[oc - 1][TH - 1])
                            for th in range(TH):
                                for kc in range(DC):
                                    ins = t.matmul(
                                        pqo[:, th * NT:(th + 1) * NT],
                                        qv_slot(j)[:, kc * 128:(kc + 1) * 128],
                                        wvT[:, kc * TOK + th * NT: kc * TOK + (th + 1) * NT],
                                        start=(kc == 0), stop=(kc == DC - 1))
                                sT.inc(ins)
                                o_done[oc][th] = sT.val
                            qv_last_use[j] = sT.val

                        @blk.vector
                        def _(v, oc=oc):
                            if oc == 0:
                                v.wait_ge(sYP.h, yp_all)
                            for th in range(TH):
                                v.wait_ge(sT.h, o_done[oc][th])
                                sl = r1[:, oc * TOK + th * NT: oc * TOK + (th + 1) * NT]
                                ins = v.scalar_tensor_tensor(
                                    sl, pqo[:, th * NT:(th + 1) * NT],
                                    biass[:, 32 + oc:33 + oc], sl,
                                    op0=ALU.add, op1=ALU.add)
                                sV.inc(ins)
                                r1_add[oc][th] = sV.val

                        for th in range(TH):
                            @blk.scalar
                            def _(s, oc=oc, th=th):
                                k = oc * TH + th
                                s.wait_ge(sV.h, r1_add[oc][th])
                                if k >= 1:
                                    s.wait_ge(sT.h, st_sum[(k - 1) // TH][(k - 1) % TH])
                                    s.wait_ge(sV.h, sq_done[(k - 1) // TH][(k - 1) % TH])
                                ins = s.activation(
                                    rbt[:, :],
                                    r1[:, oc * TOK + th * NT: oc * TOK + (th + 1) * NT],
                                    AF.Identity)
                                sS.inc(ins)
                                r1_cast[oc][th] = sS.val

                            @blk.vector
                            def _(v, oc=oc, th=th):
                                k = oc * TH + th
                                v.wait_ge(sS.h, r1_cast[oc][th])
                                if k >= 1:
                                    v.wait_ge(sT.h, st_sq[(k - 1) // TH][(k - 1) % TH])
                                rb = rbt[:, :]
                                ins = v.tensor_tensor(sqt[:, :],
                                                      rb, rb, op=ALU.mult)
                                sV.inc(ins)
                                sq_done[oc][th] = sV.val

                            @blk.tensor
                            def _(t, oc=oc, th=th):
                                t.wait_ge(sS.h, r1_cast[oc][th])
                                ins = t.matmul(pstatA[0:1, th * NT:(th + 1) * NT],
                                               onesb[:, :],
                                               rbt[:, :],
                                               start=(oc == 0), stop=(oc == DC - 1))
                                sT.inc(ins)
                                st_sum[oc][th] = sT.val
                                t.wait_ge(sV.h, sq_done[oc][th])
                                ins = t.matmul(pstatB[0:1, th * NT:(th + 1) * NT],
                                               onesb[:, :],
                                               sqt[:, :],
                                               start=(oc == 0), stop=(oc == DC - 1))
                                sT.inc(ins)
                                st_sq[oc][th] = sT.val
                                last_attn_T[0] = sT.val

                    ln1 = emit_ln_stats(ab, lnscr, pstatA, pstatB, sV, sS)

                    @blk.vector
                    def _(v):
                        v.wait_ge(sT.h, st_sq[DC - 1][TH - 1])
                        ln1[0](v)

                    @blk.scalar
                    def _(s):
                        s.wait_ge(sV.h, sV.val)
                        ln1[1](s)

                    @blk.vector
                    def _(v):
                        v.wait_ge(sS.h, sS.val)
                        ln1[2](v)
                    ln1_ab = sV.val

                    with ExitStack() as _bc:
                        pbca = _bc.enter_context(nc.psum_tensor("pbca", [128, NT], F32))
                        pbcb = _bc.enter_context(nc.psum_tensor("pbcb", [128, NT], F32))
                        bc1 = [None, None]
                        for th in range(TH):
                            @blk.tensor
                            def _(t, th=th):
                                if th == 0:
                                    t.wait_ge(sV.h, ln1_ab)
                                else:
                                    t.wait_ge(sV.h, norm1_done[DC - 1])
                                t.matmul(pbca[:, :], onesf[0:1, :],
                                         ab[0:1, th * NT:(th + 1) * NT],
                                         start=True, stop=True)
                                ins = t.matmul(pbcb[:, :], onesf[0:1, :],
                                               ab[0:1, TOK + th * NT: TOK + (th + 1) * NT],
                                               start=True, stop=True)
                                sT.inc(ins)
                                bc1[th] = sT.val

                            for c in range(DC):
                                k = th * DC + c

                                @blk.vector
                                def _(v, th=th, c=c, k=k):
                                    if c == 0:
                                        v.wait_ge(sT.h, bc1[th])
                                    slot = k % 2
                                    if k >= 2:
                                        v.wait_ge(sX1O.h, 16 * (k - 1))
                                        v.wait_ge(sS.h, x1_cast[k - 2])
                                    xs = x1t[:, slot * NT:(slot + 1) * NT]
                                    v.tensor_tensor(
                                        xs, r1[:, c * TOK + th * NT: c * TOK + (th + 1) * NT],
                                        pbca[:, :], op=ALU.mult)
                                    ins = v.tensor_tensor(xs, xs, pbcb[:, :], op=ALU.add)
                                    sV.inc(ins)
                                    norm1_done[k] = sV.val

                                @blk.scalar
                                def _(s, th=th, c=c, k=k):
                                    slot = k % 2
                                    s.wait_ge(sV.h, norm1_done[k])
                                    ins = s.activation(
                                        xb[:, c * TOK + th * NT: c * TOK + (th + 1) * NT],
                                        x1t[:, slot * NT:(slot + 1) * NT], AF.Identity)
                                    sS.inc(ins)
                                    x1_cast[k] = sS.val

                                @blk.sync
                                def _(y, th=th, c=c, k=k):
                                    slot = k % 2
                                    y.wait_ge(sV.h, norm1_done[k])
                                    y.dma_start(
                                        x1sp[c * 128:(c + 1) * 128, th * NT:(th + 1) * NT],
                                        x1t[:, slot * NT:(slot + 1) * NT]
                                    ).then_inc(sX1O.h, 16)
                                    sX1O.bump()

            if stage == "wv":
                dwv = nc.dram_tensor("dwv", [128, DC * TOK], BF16, kind="ExternalOutput").ap()
                sDBG = Ctr(nc, "sDBG", 16)

                @blk.sync
                def _(y):
                    y.wait_ge(sS.h, wv_ev[HEADS - 1])
                    y.dma_start(dwv[:, :], wvT[:, :]).then_inc(sDBG.h, 16)
                    y.wait_ge(sDBG.h, 16)

            if dbg:
                wvTd = nc.dram_tensor("wvTd", [128, DC * TOK], BF16,
                                      kind="ExternalOutput").ap()
                qtrd = nc.dram_tensor("qtrd", [128, 2 * TOK], BF16,
                                      kind="ExternalOutput").ap()
                r1d = nc.dram_tensor("r1d", [128, DC * TOK], F32,
                                     kind="ExternalOutput").ap()
                sDBG2 = Ctr(nc, "sDBG2", 16)

                @blk.sync
                def _(y):
                    y.wait_ge(sV.h, norm1_done[TH * DC - 1])
                    y.dma_start(wvTd[:, :], wvT[:, :]).then_inc(sDBG2.h, 16)
                    y.dma_start(qtrd[:, :], qtr[:, :]).then_inc(sDBG2.h, 16)
                    y.dma_start(r1d[:, :], r1[:, :]).then_inc(sDBG2.h, 16)
                    y.wait_ge(sDBG2.h, 48)

            @blk.gpsimd
            def _(g):
                for j, src in enumerate(qv_src):
                    if j >= 3:
                        g.wait_ge(sT.h, qv_last_use[j - 3])
                    g.dma_start(qv_slot(j), src.rearrange("p c o -> p (c o)")
                                ).then_inc(qvS[j % 4].h, 16)
                    qvS[j % 4].bump()
                j = len(qv_src)
                g.dma_start(latTs[:, 0:L], latT[:, 0:L]).then_inc(qvS[j % 4].h, 16)
                qvS[j % 4].bump()

        with ExitStack() as _ffn:
            ef = _ffn.enter_context
            hid = ef(nc.sbuf_tensor("hid", [128, HC * TOK], BF16))
            silut = ef(nc.sbuf_tensor("silut", [128, 2 * NT], F32))
            bsp = ef(nc.sbuf_tensor("bsp", [128, TOK], BF16))
            r2r = ef(nc.sbuf_tensor("r2r", [128, 4 * NT], F32))
            r2b = ef(nc.sbuf_tensor("r2b", [128, NT], BF16))
            sq2 = ef(nc.sbuf_tensor("sq2", [128, NT], BF16))
            def w13_slot(j):
                return w13w[:, (j % 4) * D:((j % 4) + 1) * D]

            def w2_slot(j):
                return w2w[:, (j % 3) * 1024:((j % 3) + 1) * 1024]

            w13_last_use = [None] * (2 * HC)
            w2_last_use = [None] * (2 * DC * 8)

            up_ev = [[None] * TH for _ in range(HC)]
            ug_done = [[None] * TH for _ in range(HC)]
            ul_done = [[None] * TH for _ in range(HC)]
            silu_done = [[None] * TH for _ in range(HC)]

            smap = [list(range(HC)) for _ in range(TH)]
            spare = [HC, HC + 1]

            def hslice(phys, th):
                if phys < HC:
                    return hid[:, phys * TOK + th * NT: phys * TOK + (th + 1) * NT]
                u = phys - HC
                return bsp[:, u * NT:(u + 1) * NT]

            bstage_done = [None, None]
            bf_final = [[0] * HC for _ in range(TH)]
            stage_state = [
                {"stage": 0, "idx": 0, "done": [set() for _ in range(len(BST) + 1)]}
                for _ in range(TH)]

            def emit_ready_butterflies(v, th):
                st = stage_state[th]
                while st["stage"] < len(BST):
                    s_i = st["stage"]
                    pairs = BST[s_i]
                    progressed = False
                    while st["idx"] < len(pairs):
                        a, b = pairs[st["idx"]]
                        if not (a in st["done"][s_i] and b in st["done"][s_i]):
                            break
                        pa, pb = smap[th][a], smap[th][b]
                        sp = spare[th]
                        v.tensor_tensor(hslice(sp, th), hslice(pa, th),
                                        hslice(pb, th), op=ALU.add)
                        ins = v.tensor_tensor(hslice(pb, th), hslice(pa, th),
                                              hslice(pb, th), op=ALU.subtract)
                        sV.inc(ins)
                        smap[th][a] = sp
                        spare[th] = pa
                        bf_final[th][a] = sV.val
                        bf_final[th][b] = sV.val
                        st["done"][s_i + 1].add(a)
                        st["done"][s_i + 1].add(b)
                        st["idx"] += 1
                        progressed = True
                    if st["idx"] == len(pairs):
                        st["stage"] += 1
                        st["idx"] = 0
                        if st["stage"] == len(BST):
                            bstage_done[th] = sV.val
                        continue
                    if not progressed:
                        break

            w2_done = [[None] * DC for _ in range(TH)]
            r2_add = [[None] * DC for _ in range(TH)]
            r2_cast = [[None] * DC for _ in range(TH)]
            sq2_done = [[None] * DC for _ in range(TH)]
            st2_sum = [[None] * DC for _ in range(TH)]
            st2_sq = [[None] * DC for _ in range(TH)]
            stageA_done = [[None] * HC for _ in range(TH)]

            with ExitStack() as _st2:
                pstat2A = _st2.enter_context(nc.psum_tensor("pstat2A", [1, TOK], F32))
                pstat2B = _st2.enter_context(nc.psum_tensor("pstat2B", [1, TOK], F32))
                with ExitStack() as _up:
                    pug = _up.enter_context(nc.psum_tensor("pug", [128, TOK], F32))
                    pul = _up.enter_context(nc.psum_tensor("pul", [128, TOK], F32))
                    for hc in range(HC):
                        @blk.tensor
                        def _(t, hc=hc):
                            j1, j3 = 2 * hc, 2 * hc + 1
                            t.wait_ge(w13S[j1 % 4].h, 16 * (j1 // 4 + 1))
                            if hc >= 1:
                                t.wait_ge(sV.h, up_ev[hc - 1][TH - 1])
                            for th in range(TH):
                                for kc in range(DC):
                                    if hc == 0:
                                        t.wait_ge(sS.h, x1_cast[th * DC + kc])
                                    ins = t.matmul(
                                        pug[:, th * NT:(th + 1) * NT],
                                        w13_slot(j1)[:, kc * 128:(kc + 1) * 128],
                                        xb[:, kc * TOK + th * NT: kc * TOK + (th + 1) * NT],
                                        start=(kc == 0), stop=(kc == DC - 1))
                                sT.inc(ins)
                                ug_done[hc][th] = sT.val
                            w13_last_use[j1] = sT.val
                            t.wait_ge(w13S[j3 % 4].h, 16 * (j3 // 4 + 1))
                            for th in range(TH):
                                for kc in range(DC):
                                    ins = t.matmul(
                                        pul[:, th * NT:(th + 1) * NT],
                                        w13_slot(j3)[:, kc * 128:(kc + 1) * 128],
                                        xb[:, kc * TOK + th * NT: kc * TOK + (th + 1) * NT],
                                        start=(kc == 0), stop=(kc == DC - 1))
                                sT.inc(ins)
                                ul_done[hc][th] = sT.val
                            w13_last_use[j3] = sT.val

                        @blk.scalar
                        def _(s, hc=hc):
                            for th in range(TH):
                                k = hc * TH + th
                                slot = k % 2
                                s.wait_ge(sT.h, ug_done[hc][th])
                                if k >= 2:
                                    pr = k - 2
                                    s.wait_ge(sV.h, up_ev[pr // TH][pr % TH])
                                ins = s.activation(silut[:, slot * NT:(slot + 1) * NT],
                                                   pug[:, th * NT:(th + 1) * NT], AF.Silu)
                                sS.inc(ins)
                                silu_done[hc][th] = sS.val

                        @blk.vector
                        def _(v, hc=hc):
                            for th in range(TH):
                                slot = (hc * TH + th) % 2
                                v.wait_ge(sS.h, silu_done[hc][th])
                                v.wait_ge(sT.h, ul_done[hc][th])
                                ins = v.tensor_tensor(
                                    hid[:, hc * TOK + th * NT: hc * TOK + (th + 1) * NT],
                                    silut[:, slot * NT:(slot + 1) * NT],
                                    pul[:, th * NT:(th + 1) * NT], op=ALU.mult)
                                sV.inc(ins)
                                up_ev[hc][th] = sV.val
                                stage_state[th]["done"][0].add(hc)
                                emit_ready_butterflies(v, th)

                with ExitStack() as _aw:
                    pa = _aw.enter_context(nc.psum_tensor("pa", [128, TOK], F32))
                    pw2 = _aw.enter_context(nc.psum_tensor("pw2", [128, TOK], F32))
                    for th in range(TH):
                        for blkk in range(8):
                            for q in range(8):
                                h1 = blkk * 8 + q

                                @blk.tensor
                                def _(t, th=th, h1=h1):
                                    phys = smap[th][h1]
                                    if h1 == 0 and th == 0:
                                        t.wait_ge(sH128.h, 16)
                                    t.wait_ge(sV.h, bf_final[th][h1])
                                    if h1 >= 2:
                                        t.wait_ge(sS.h, stageA_done[th][h1 - 2])
                                    ins = t.matmul(
                                        pa[:, (h1 % 2) * NT:((h1 % 2) + 1) * NT],
                                        h128s[:, :], hslice(phys, th),
                                        start=True, stop=True)
                                    sT.inc(ins)

                                @blk.scalar
                                def _(s, th=th, h1=h1):
                                    phys = smap[th][h1]
                                    s.wait_ge(sT.h, sT.val)
                                    ins = s.activation(
                                        hslice(phys, th),
                                        pa[:, (h1 % 2) * NT:((h1 % 2) + 1) * NT],
                                        AF.Identity, scale=RSQ_HID)
                                    sS.inc(ins)
                                    stageA_done[th][h1] = sS.val

                            @blk.tensor
                            def _(t, th=th, blkk=blkk):
                                if blkk == 0 and th >= 1:
                                    t.wait_ge(sV.h, r2_add[th - 1][DC - 1])
                                j = th * DC * 8 + blkk
                                t.wait_ge(w2S[j % 3].h, 16 * (j // 3 + 1))
                                t.wait_ge(sS.h, stageA_done[th][blkk * 8 + 7])
                                for q in range(8):
                                    h1 = blkk * 8 + q
                                    ins = t.matmul(
                                        pw2[:, th * NT:(th + 1) * NT],
                                        w2_slot(j)[:, q * 128:(q + 1) * 128],
                                        hslice(smap[th][h1], th),
                                        start=(h1 == 0), stop=(h1 == HC - 1))
                                sT.inc(ins)
                                w2_last_use[j] = sT.val
                                if blkk == 7:
                                    w2_done[th][0] = sT.val
                                    k = th * DC
                                    if k >= 1:
                                        emit_stats(t, (k - 1) // DC, (k - 1) % DC)

                        def emit_stats(t, pth, pdc):
                            t.wait_ge(sS.h, r2_cast[pth][pdc])
                            ins = t.matmul(pstat2A[0:1, pth * NT:(pth + 1) * NT],
                                           onesb[:, :],
                                           r2b[:, :],
                                           start=(pdc == 0), stop=(pdc == DC - 1))
                            sT.inc(ins)
                            st2_sum[pth][pdc] = sT.val
                            t.wait_ge(sV.h, sq2_done[pth][pdc])
                            ins = t.matmul(pstat2B[0:1, pth * NT:(pth + 1) * NT],
                                           onesb[:, :],
                                           sq2[:, :],
                                           start=(pdc == 0), stop=(pdc == DC - 1))
                            sT.inc(ins)
                            st2_sq[pth][pdc] = sT.val

                        for dc in range(DC):
                            if dc >= 1:
                                @blk.tensor
                                def _(t, th=th, dc=dc):
                                    t.wait_ge(sV.h, r2_add[th][dc - 1])
                                    for blkk in range(8):
                                        j = th * DC * 8 + dc * 8 + blkk
                                        t.wait_ge(w2S[j % 3].h, 16 * (j // 3 + 1))
                                        for q in range(8):
                                            h1 = blkk * 8 + q
                                            ins = t.matmul(
                                                pw2[:, th * NT:(th + 1) * NT],
                                                w2_slot(j)[:, q * 128:(q + 1) * 128],
                                                hslice(smap[th][h1], th),
                                                start=(h1 == 0), stop=(h1 == HC - 1))
                                        sT.inc(ins)
                                        w2_last_use[j] = sT.val
                                    w2_done[th][dc] = sT.val
                                    k = th * DC + dc
                                    emit_stats(t, (k - 1) // DC, (k - 1) % DC)

                            @blk.sync
                            def _(y, th=th, dc=dc):
                                k = th * DC + dc
                                u = k % 4
                                if k == 0:
                                    y.wait_ge(sX1O.h, 16 * TH * DC)
                                    y.wait_ge(sT.h, last_attn_T[0])
                                    y.wait_ge(sV.h, norm1_done[TH * DC - 1])
                                if k >= 4:
                                    pr = k - 4
                                    y.wait_ge(sS.h, r2_cast[pr // DC][pr % DC])
                                    y.wait_ge(sR2O.h, 16 * (pr + 1))
                                y.dma_start(
                                    r2r[:, u * NT:(u + 1) * NT],
                                    x1sp[dc * 128:(dc + 1) * 128, th * NT:(th + 1) * NT]
                                ).then_inc(x1iS[u].h, 16)
                                x1iS[u].bump()

                            @blk.vector
                            def _(v, th=th, dc=dc):
                                u = (th * DC + dc) % 4
                                v.wait_ge(x1iS[u].h, x1iS[u].val)
                                v.wait_ge(sT.h, w2_done[th][dc])
                                sl = r2r[:, u * NT:(u + 1) * NT]
                                ins = v.tensor_tensor(sl, sl,
                                                      pw2[:, th * NT:(th + 1) * NT],
                                                      op=ALU.add)
                                sV.inc(ins)
                                r2_add[th][dc] = sV.val

                            @blk.scalar
                            def _(s, th=th, dc=dc):
                                k = th * DC + dc
                                u = k % 4
                                s.wait_ge(sV.h, r2_add[th][dc])
                                if k >= 1:
                                    pr = k - 1
                                    s.wait_ge(sT.h, st2_sum[pr // DC][pr % DC])
                                    s.wait_ge(sV.h, sq2_done[pr // DC][pr % DC])
                                ins = s.activation(r2b[:, :],
                                                   r2r[:, u * NT:(u + 1) * NT],
                                                   AF.Identity)
                                sS.inc(ins)
                                r2_cast[th][dc] = sS.val

                            @blk.vector
                            def _(v, th=th, dc=dc):
                                k = th * DC + dc
                                v.wait_ge(sS.h, r2_cast[th][dc])
                                if k >= 1:
                                    pr = k - 1
                                    v.wait_ge(sT.h, st2_sq[pr // DC][pr % DC])
                                rb = r2b[:, :]
                                ins = v.tensor_tensor(sq2[:, :],
                                                      rb, rb, op=ALU.mult)
                                sV.inc(ins)
                                sq2_done[th][dc] = sV.val

                            @blk.sync
                            def _(y, th=th, dc=dc):
                                u = (th * DC + dc) % 4
                                y.wait_ge(sS.h, r2_cast[th][dc])
                                y.dma_start(
                                    r2sp[dc * 128:(dc + 1) * 128, th * NT:(th + 1) * NT],
                                    r2r[:, u * NT:(u + 1) * NT]).then_inc(sR2O.h, 16)
                                sR2O.bump()

                    @blk.tensor
                    def _(t):
                        emit_stats(t, TH - 1, DC - 1)

                ln2 = emit_ln_stats(ab, r2r[0:1, 0:2 * TOK], pstat2A, pstat2B, sV, sS)

                @blk.vector
                def _(v):
                    v.wait_ge(sT.h, st2_sq[TH - 1][DC - 1])
                    ln2[0](v)

                @blk.scalar
                def _(s):
                    s.wait_ge(sV.h, sV.val)
                    ln2[1](s)

                @blk.vector
                def _(v):
                    v.wait_ge(sS.h, sS.val)
                    ln2[2](v)
                ln2_ab = sV.val

                fin_done = [None] * DC
                with ExitStack() as _bc2:
                    pbca2 = _bc2.enter_context(nc.psum_tensor("pbca2", [128, TOK], F32))
                    pbcb2 = _bc2.enter_context(nc.psum_tensor("pbcb2", [128, TOK], F32))
                    @blk.tensor
                    def _(t):
                        t.wait_ge(sV.h, ln2_ab)
                        for th in range(TH):
                            t.matmul(pbca2[:, th * NT:(th + 1) * NT], onesf[0:1, :],
                                     ab[0:1, th * NT:(th + 1) * NT],
                                     start=True, stop=True)
                            ins = t.matmul(pbcb2[:, th * NT:(th + 1) * NT],
                                           onesf[0:1, :],
                                           ab[0:1, TOK + th * NT: TOK + (th + 1) * NT],
                                           start=True, stop=True)
                        sT.inc(ins)
                    bc2 = sT.val

                    r2in_val = [None] * DC
                    for c in range(DC):
                        @blk.sync
                        def _(y, c=c):
                            u = c % 2
                            if c == 0:
                                y.wait_ge(sR2O.h, 16 * TH * DC)
                                y.wait_ge(sV.h, ln2_ab)
                            if c >= 2:
                                y.wait_ge(sV.h, fin_done[c - 2])
                                y.wait_ge(sOUT.h, 16 * (c - 1))
                            y.dma_start(r2r[:, u * TOK:(u + 1) * TOK],
                                        r2sp[c * 128:(c + 1) * 128, :]
                                        ).then_inc(r2iS[u].h, 16)
                            r2iS[u].bump()
                            r2in_val[c] = r2iS[u].val

                        @blk.vector
                        def _(v, c=c):
                            u = c % 2
                            if c == 0:
                                v.wait_ge(sT.h, bc2)
                            v.wait_ge(r2iS[u].h, r2in_val[c])
                            fs = r2r[:, u * TOK:(u + 1) * TOK]
                            v.tensor_tensor(fs, fs, pbca2[:, :], op=ALU.mult)
                            ins = v.tensor_tensor(fs, fs, pbcb2[:, :], op=ALU.add)
                            sV.inc(ins)
                            fin_done[c] = sV.val

                        @blk.sync
                        def _(y, c=c):
                            u = c % 2
                            y.wait_ge(sV.h, fin_done[c])
                            y.dma_start(outT[c * 128:(c + 1) * 128, :],
                                        r2r[:, u * TOK:(u + 1) * TOK]
                                        ).then_inc(sOUT.h, 16)
                            sOUT.bump()

                    @blk.sync
                    def _(y):
                        y.wait_ge(sOUT.h, 16 * DC)

            @blk.gpsimd
            def _(g):
                for j in range(2 * HC):
                    hc, is3 = j // 2, j % 2
                    src = (w3 if is3 else w1)[hc]
                    if j >= 4:
                        g.wait_ge(sT.h, w13_last_use[j - 4])
                    g.dma_start(w13_slot(j), src.rearrange("p c o -> p (c o)")
                                ).then_inc(w13S[j % 4].h, 16)
                    w13S[j % 4].bump()
                j = 2 * HC
                g.dma_start(h128s[:, :], h128h[:, :]).then_inc(w13S[j % 4].h, 16)
                w13S[j % 4].bump()
                for j in range(2 * DC * 8):
                    th = j // (DC * 8)
                    dc = (j // 8) % DC
                    blkk = j % 8
                    src = w2[dc, :, blkk * 8:(blkk + 1) * 8, :]
                    if j >= 3:
                        g.wait_ge(sT.h, w2_last_use[j - 3])
                    g.dma_start(w2_slot(j), src.rearrange("p c o -> p (c o)")
                                ).then_inc(w2S[j % 3].h, 16)
                    w2S[j % 3].bump()
                j = 2 * DC * 8
                g.dma_start(h128s[:, :], h128h[:, :]).then_inc(w2S[j % 3].h, 16)
                w2S[j % 3].bump()

    return nc


def _prep_w(w, ocn, kcn):
    return np.ascontiguousarray(
        w.reshape(ocn, 128, kcn, 128).transpose(0, 3, 2, 1)).astype(np.float32)


def kernel(**inputs):
    x = np.asarray(inputs["L0_attn_hidden_states"], dtype=np.float32)
    Wq = np.asarray(inputs["L0_attn_Wq"], dtype=np.float32)
    bq = np.asarray(inputs["L0_attn_bq"], dtype=np.float32)
    Wv = np.asarray(inputs["L0_attn_Wv"], dtype=np.float32)
    bv = np.asarray(inputs["L0_attn_bv"], dtype=np.float32)
    Wo = np.asarray(inputs["L0_attn_Wo"], dtype=np.float32)
    bo = np.asarray(inputs["L0_attn_bo"], dtype=np.float32)
    lat = np.asarray(inputs["L0_attn_latent_keys"], dtype=np.float32)
    W1 = np.asarray(inputs["L0_ffn_W1"], dtype=np.float32)
    W2 = np.asarray(inputs["L0_ffn_W2"], dtype=np.float32)
    W3 = np.asarray(inputs["L0_ffn_W3"], dtype=np.float32)

    X = x.reshape(NCORES * TOK, D)
    wq_t = _prep_w(Wq, DC, DC)
    wv_t = _prep_w(Wv, DC, DC)
    wo_t = _prep_w(Wo, DC, DC)
    w1_t = _prep_w(W1, HC, DC)
    w3_t = _prep_w(W3, HC, DC)
    w2_t = _prep_w(W2, DC, HC)
    latT = np.ascontiguousarray(
        lat.reshape(L, DC, 128).transpose(2, 1, 0).reshape(128, DC * L)).astype(np.float32)
    bia = np.ascontiguousarray(
        np.stack([bq, bv, bo], 0).reshape(3, DC, 128).transpose(2, 0, 1)
        .reshape(128, 48)).astype(np.float32)
    h128 = hadamard(128).astype(bfdt)
    ident = np.eye(128, dtype=bfdt)
    onesb = np.ones((128, 1), dtype=bfdt)
    onesf = np.ones((1, 128), dtype=np.float32)

    nc = build_nc()
    in_maps = []
    for i in range(NCORES):
        xi = np.ascontiguousarray(X[i * TOK:(i + 1) * TOK].T)
        in_maps.append({
            "xT": xi, "wq": wq_t, "wv": wv_t, "wo": wo_t,
            "w1": w1_t, "w3": w3_t, "w2": w2_t,
            "latT": latT, "latN": lat, "bia": bia,
            "h128c": h128, "identc": ident, "onesbc": onesb, "onesfc": onesf,
        })
    res = run_bass_kernel_spmd(nc, in_maps, core_ids=list(range(NCORES)))
    outs = [np.asarray(r["outT"]).T for r in res.results]
    return np.concatenate(outs, axis=0).reshape(2, 4096, D).astype(np.float32)
